# version 13
# baseline (speedup 1.0000x reference)
import numpy as np
import concourse.bass as bass
import concourse.mybir as mybir
from concourse.bass_utils import run_bass_kernel_spmd
from contextlib import ExitStack

F32 = mybir.dt.float32
BF16 = mybir.dt.bfloat16
AF = mybir.ActivationFunctionType
ALU = mybir.AluOpType

D = 2048
T = 4096
TOWN = 2048
H = 16
ND = 16
NS = 32
NTB = 4
NI = 44
NIG = 11
CHA = 256
NCHA = 16
INV_SQRT_T = 1.0 / 64.0
EPS = 1e-5

DEBUG_DUMPS = False
DEBUG_SET = None
DEBUG_SERIAL_DMA = False

ENGS = ("sync", "tensor", "vector", "scalar", "gpsimd")
DMAQ = {"qc", "qdbg", "qx0", "qx1", "qw0", "qw1", "qt0", "qt1", "qm",
        "qk0", "qk1", "qv0", "qv1", "qq0", "qq1", "qo0", "qo1",
        "qg0", "qg1", "qu0", "qu1", "qd0", "qd1", "qxo", "qout",
        "qwo0", "qwo1", "qwg0", "qwg1", "qwu0", "qwu1", "qwd0", "qwd1",
        "qb0", "qb1", "qcw0", "qcw1"}
SEMS = ("pe", "dve", "act", "pool") + tuple(sorted(DMAQ))


class Rec:
    def __init__(self):
        self.items = {e: [] for e in ENGS}
        self.cnt = {s: 0 for s in SEMS}
        self.hw = {e: {} for e in ENGS}

    def wait(self, eng, ev):
        if ev is None:
            return
        sem, val = ev
        if val <= 0:
            return
        if self.hw[eng].get(sem, 0) >= val:
            return
        self.hw[eng][sem] = val
        self.items[eng].append(("w", sem, val))

    def op(self, eng, fn, waits=(), inc=None):
        for ev in waits:
            self.wait(eng, ev)
        if DEBUG_SERIAL_DMA and inc in DMAQ:
            for s_ in DMAQ:
                self.wait(eng, (s_, self.cnt[s_]))
        amt = 0
        if inc is not None:
            amt = 16 if inc in DMAQ else 1
            self.cnt[inc] += amt
        self.items[eng].append(("o", fn, inc, amt))
        return (inc, self.cnt[inc]) if inc else None

    def mark(self, sem):
        return (sem, self.cnt[sem])


class Ctx:
    pass


def build_nc():
    nc = bass.Bass("TRN2", target_bir_lowering=False)
    c = Ctx()
    c.nc = nc
    c.r = Rec()

    dts = {}
    for name, shape, dt, kind in [
        ("xT", [D, T], F32, "ExternalInput"),
        ("maskT", [T, TOWN], F32, "ExternalInput"),
        ("wq", [D, D], F32, "ExternalInput"),
        ("wk", [D, D], F32, "ExternalInput"),
        ("wv", [D, D], F32, "ExternalInput"),
        ("wo", [D, D], F32, "ExternalInput"),
        ("gwT", [D, 5632], F32, "ExternalInput"),
        ("uwT", [D, 5632], F32, "ExternalInput"),
        ("dwT", [5632, D], F32, "ExternalInput"),
        ("gs2", [128, NI], F32, "ExternalInput"),
        ("us2", [128, NI], F32, "ExternalInput"),
        ("ds2", [128, ND], F32, "ExternalInput"),
        ("outT", [D, TOWN], F32, "ExternalOutput"),
        ("kscr", [D, T], BF16, "Internal"),
        ("woc", [D, D], BF16, "Internal"),
        ("gwc", [D, 5632], BF16, "Internal"),
        ("uwc", [D, 5632], BF16, "Internal"),
        ("dwc", [5632, D], BF16, "Internal"),
        ("vscr", [T, D], BF16, "Internal"),
        ("qscr", [D, TOWN], BF16, "Internal"),
    ]:
        dts[name] = nc.dram_tensor(name, shape, dt, kind=kind)
    if DEBUG_DUMPS:
        for name, shape, dt in [
            ("dxln", [128, ND * T], BF16),
            ("dkscr", [D, T], BF16),
            ("dvscr", [T, D], BF16),
            ("dqscr", [D, TOWN], BF16),
            ("daall", [128, H * 512], BF16),
            ("dao", [128, ND * 512], F32),
            ("dy", [128, ND * 512], F32),
            ("dz", [128, ND * 512], BF16),
            ("dh", [128, NI * 512], BF16),
            ("df", [128, ND * 512], F32),
            ("dpt", [128, NS * 512], BF16),
            ("dmrow", [1, 512], F32),
            ("drrow", [1, 512], F32),
        ]:
            dts[name] = nc.dram_tensor(name, shape, dt, kind="ExternalOutput")
    c.dram = dts

    base = ((nc.sbuf_base + 63) // 64) * 64
    off = base

    def take(n):
        nonlocal off
        a = off
        off += (n + 63) // 64 * 64
        return a

    o_consts = take(2368)
    o_rows = take(4096)
    o_small = take(6144)
    o_tmp = take(8192)
    o_zb = take(2048)
    o_aall = take(16384)
    o_big = take(131072)
    o_w = take(32768)
    o_r6 = take(8192)
    assert off <= nc.sbuf_top, f"SBUF overflow: {off} > {nc.sbuf_top}"

    A = nc.alloc_sbuf_tensor_at
    c.onesc_f = A("onesc_f", [128, 1], F32, offset=o_consts)
    c.onesr_f = A("onesr_f", [1, 128], F32, offset=o_consts + 64)
    c.ones_bf = A("ones_bf", [128, 1], BF16, offset=o_consts + 576)
    c.gs2_sb = A("gs2_sb", [128, NI], F32, offset=o_consts + 640)
    c.us2_sb = A("us2_sb", [128, NI], F32, offset=o_consts + 832)
    c.ds2_sb = A("ds2_sb", [128, ND], F32, offset=o_consts + 1024)
    c.onesrow_bf = A("onesrow_bf", [1, 512], BF16, offset=o_consts + 1088)
    c.r_R = A("r_R", [1, 128], BF16, offset=o_consts + 2112)
    rows_a = A("rows_a", [128, 512], F32, offset=o_rows)
    rows_b = A("rows_b", [128, 512], F32, offset=o_rows + 2048)
    c.r_m = rows_a[0:1, :]
    c.r_q = rows_a[32:33, :]
    c.r_var = rows_b[32:33, :]
    c.r_rstd = rows_b[0:1, :]
    c.r_z = rows_a[0:1, :]
    c.kstage = [A(f"kstage{i}", [128, 512], BF16, offset=o_small + i * 1024) for i in range(2)]
    c.qh = [A(f"qh{i}", [128, 512], BF16, offset=o_small + i * 1024) for i in range(2)]
    c.sq = [A(f"sq{i}", [128, 512], BF16, offset=o_small + 2048 + i * 1024) for i in range(2)]
    c.xbf = [A(f"xbf{i}", [128, 512], BF16, offset=o_small + 4096 + i * 1024) for i in range(2)]
    c.sm = [A(f"sm{i}", [128, 512], F32, offset=o_tmp + i * 2048) for i in range(4)]
    c.t1 = [A(f"t1{i}", [128, 512], F32, offset=o_tmp + i * 2048) for i in range(2)]
    c.sg = [A(f"sg{i}", [128, 512], F32, offset=o_tmp + i * 2048) for i in range(4)]
    c.zb = A("zb", [128, 512], F32, offset=o_zb)
    c.a_all = A("a_all", [128, H * 512], BF16, offset=o_aall)
    c.gbig = [A(f"gbig{i}", [128, 4 * 512], BF16, offset=o_aall + i * 4096) for i in range(2)]
    c.ubig = [A(f"ubig{i}", [128, 4 * 512], BF16, offset=o_aall + 8192 + i * 4096) for i in range(2)]
    c.xln = A("xln", [128, ND * T], BF16, offset=o_big)
    c.mask = A("mask", [128, NS * 512], F32, offset=o_big)
    c.pt = A("pt", [128, NS * 512], BF16, offset=o_big + 65536)
    c.kh = [A(f"kh{i}", [128, T], BF16, offset=o_big + 98304 + i * 8192) for i in range(2)]
    c.vh = [A(f"vh{i}", [128, T], BF16, offset=o_big + 114688 + i * 8192) for i in range(2)]
    c.h_all = A("h_all", [128, NI * 512], BF16, offset=o_big)
    c.z_all = A("z_all", [128, ND * 512], BF16, offset=o_big + 45056)
    c.ao_all = A("ao_all", [128, ND * 512], F32, offset=o_big + 65536)
    c.xc_own = A("xc_own", [128, ND * 512], F32, offset=o_big + 98304)
    c.out_all = A("out_all", [128, ND * 512], F32, offset=o_big + 98304)
    c.wres = [A("wres0", [128, ND * 512], BF16, offset=o_aall),
              A("wres1", [128, ND * 512], BF16, offset=o_w)]
    c.xc = [A(f"xc{i}", [128, ND * CHA], F32, offset=o_w + i * 16384) for i in range(2)]
    c.y_all = A("y_all", [128, ND * 512], F32, offset=o_w)
    c.wob = [A(f"wob{i}", [128, 1024], BF16, offset=o_r6 + i * 2048) for i in range(2)]
    c.dbig = [A(f"dbig{i}", [128, 2 * 512], BF16, offset=o_r6 + 4096 + i * 2048) for i in range(2)]

    c.pb = [nc.alloc_psum_tensor(f"pb{i}", [128, 512], F32) for i in range(8)]

    record_program(c)

    sems = {}
    with ExitStack() as stack:
        for s in SEMS:
            sems[s] = stack.enter_context(nc.semaphore(s))
        with nc.Block() as blk:
            def mk(ename):
                def f(E):
                    for it in c.r.items[ename]:
                        if it[0] == "w":
                            E.wait_ge(sems[it[1]], it[2])
                        else:
                            inst = it[1](E)
                            if it[2] is not None:
                                inst.then_inc(sems[it[2]], it[3])
                return f
            blk.sync(mk("sync"))
            blk.tensor(mk("tensor"))
            blk.vector(mk("vector"))
            blk.scalar(mk("scalar"))
            blk.gpsimd(mk("gpsimd"))
    return nc



def dma(c, eng, sem, out_ap, in_ap, waits=()):
    def fn(E, o=out_ap, i=in_ap):
        return E.dma_start(o, i)
    return c.r.op(eng, fn, waits=waits, inc=sem)


def mm(c, out, lhsT, rhs, start, stop, waits=(), inc=False):
    def fn(E, o=out, l=lhsT, rr=rhs, st=start, sp=stop):
        return E.matmul(o, l, rr, start=st, stop=sp)
    return c.r.op("tensor", fn, waits=waits, inc="pe" if inc else None)


def vop(c, name, *args, waits=(), inc=False, **kw):
    def fn(E, n=name, a=args, k=kw):
        return getattr(E, n)(*a, **k)
    return c.r.op("vector", fn, waits=waits, inc="dve" if inc else None)


def sop(c, name, *args, waits=(), inc=False, **kw):
    def fn(E, n=name, a=args, k=kw):
        return getattr(E, n)(*a, **k)
    return c.r.op("scalar", fn, waits=waits, inc="act" if inc else None)


def pop(c, name, *args, waits=(), inc=False, **kw):
    def fn(E, n=name, a=args, k=kw):
        return getattr(E, n)(*a, **k)
    return c.r.op("gpsimd", fn, waits=waits, inc="pool" if inc else None)


def W(*evs):
    return [e for e in evs if e is not None]



def ln_stats(c, tiles, n, sum_bank, sq_bank, mbc_bank, rbc_bank,
             tile_waits=None, war_banks=(), war_bc=(), inv_n=1.0 / 2048.0,
             sq_user_prev=(None, None)):
    r = c.r
    nt = len(tiles)
    ev_sqmm = list(sq_user_prev)
    evs = evq = None
    for j, tile in enumerate(tiles):
        tw = list(tile_waits[j]) if tile_waits else []
        b = j % 2
        ev_xb_j = pop(c, "tensor_copy", c.xbf[b][:, 0:n], tile,
                      waits=tw + W(ev_sqmm[b]), inc=True)
        evs = mm(c, c.pb[sum_bank][0:1, 0:n], c.ones_bf[:], c.xbf[b][:, 0:n],
                 start=(j == 0), stop=(j == nt - 1),
                 waits=[ev_xb_j] + (list(war_banks) if j == 0 else []),
                 inc=(j == nt - 1))
        ev_sq_j = pop(c, "tensor_mul", c.sq[b][:, 0:n], tile, tile,
                      waits=tw + W(ev_sqmm[b]), inc=True)
        evq = mm(c, c.pb[sq_bank][0:1, 0:n], c.ones_bf[:], c.sq[b][:, 0:n],
                 start=(j == 0), stop=(j == nt - 1),
                 waits=[ev_sq_j] + (list(war_banks) if j == 0 else []),
                 inc=True)
        ev_sqmm[b] = evq
    e1 = vop(c, "tensor_scalar_mul", c.r_m[0:1, 0:n], c.pb[sum_bank][0:1, 0:n],
             inv_n, waits=[evs], inc=True)
    e2 = vop(c, "tensor_scalar_mul", c.r_q[0:1, 0:n], c.pb[sq_bank][0:1, 0:n],
             inv_n, waits=[evq], inc=True)
    e3 = vop(c, "tensor_mul", c.r_var[0:1, 0:n], c.r_m[0:1, 0:n],
             c.r_m[0:1, 0:n], waits=[e1], inc=True)
    e4 = vop(c, "tensor_sub", c.r_var[0:1, 0:n], c.r_q[0:1, 0:n],
             c.r_var[0:1, 0:n], waits=[e2, e3], inc=True)
    e4b = vop(c, "tensor_scalar_add", c.r_var[0:1, 0:n], c.r_var[0:1, 0:n],
              EPS, waits=[e4], inc=True)
    e5 = sop(c, "activation", c.r_rstd[0:1, 0:n], c.r_var[0:1, 0:n], AF.Ln,
             waits=[e4b], inc=True)
    e6 = sop(c, "activation", c.r_rstd[0:1, 0:n], c.r_rstd[0:1, 0:n], AF.Exp,
             scale=-0.5, waits=[e5], inc=True)
    ev_mbc = mm(c, c.pb[mbc_bank][:, 0:n], c.onesr_f[:], c.r_m[0:1, 0:n],
                start=True, stop=True, waits=[e1] + list(war_bc), inc=True)
    ev_rbc = mm(c, c.pb[rbc_bank][:, 0:n], c.onesr_f[:], c.r_rstd[0:1, 0:n],
                start=True, stop=True, waits=[e6] + list(war_bc), inc=True)
    return ev_mbc, ev_rbc, e2, tuple(ev_sqmm)



def dbg_dump(c, name, src_ap, waits):
    if DEBUG_SET is not None and name not in DEBUG_SET:
        return
    ev = dma(c, "sync", "qdbg", c.dram[name].ap(), src_ap, waits=list(waits))
    for e in ("vector", "scalar", "tensor", "gpsimd", "sync"):
        c.r.wait(e, ev)


def record_program(c):
    d = c.dram
    r = c.r

    pop(c, "memset", c.onesc_f[:], 1.0)
    pop(c, "memset", c.onesr_f[:], 1.0)
    pop(c, "memset", c.onesrow_bf[:], 1.0)
    ev_ones = pop(c, "memset", c.ones_bf[:], 1.0, inc=True)
    r.wait("tensor", ev_ones)
    ev_gs = dma(c, "sync", "qc", c.gs2_sb[:], d["gs2"].ap())
    ev_us = dma(c, "sync", "qc", c.us2_sb[:], d["us2"].ap())
    ev_ds = dma(c, "sync", "qc", c.ds2_sb[:], d["ds2"].ap())
    ev_gs = ev_us = ev_ds = ev_ds

    xT3 = d["xT"].ap().rearrange("(j p) t -> p j t", p=128)

    ev_cast = {}
    ev_cast["wo"] = dma(c, "gpsimd", "qb0", d["woc"].ap(), d["wo"].ap())

    ev_apply = [None] * NCHA
    ev_chunk_rd = [None, None]
    ev_t1 = [None, None]
    ev_ln_bc_user = None
    ev_rows_prev = None
    sq_user = (None, None)
    ln1_state = {"sq_user": sq_user}

    def emit_ln1_chunk(ch):
        nonlocal ev_ln_bc_user, ev_rows_prev
        b = ch % 2
        t0 = ch * CHA
        w = []
        if ch >= 2:
            w += W(ev_apply[ch - 2]) + list(ev_chunk_rd[b] or [])
        ev_x = dma(c, "sync", f"qx{b}", c.xc[b][:].rearrange("p (j t) -> p j t", j=ND),
                   xT3[:, :, t0:t0 + CHA], waits=w)
        tiles = [c.xc[b][:, j * CHA:(j + 1) * CHA] for j in range(ND)]
        ev_mbc, ev_rbc, ev_rows1, ln1_state["sq_user"] = ln_stats(
            c, tiles, CHA, sum_bank=4, sq_bank=5, mbc_bank=2, rbc_bank=3,
            tile_waits=[[ev_x]] * ND,
            war_banks=W(ev_rows_prev) if ch >= 1 else [],
            war_bc=W(ev_ln_bc_user) + ([ev_ones] if ch == 0 else []),
            sq_user_prev=ln1_state["sq_user"])
        ev_rows_prev = ev_rows1
        ev_chunk_rd[b] = [r.mark("pe"), r.mark("act"), r.mark("pool")]
        last = None
        for j in range(ND):
            p_ = j % 2
            et = vop(c, "tensor_sub", c.t1[p_][:, 0:CHA],
                     c.xc[b][:, j * CHA:(j + 1) * CHA], c.pb[2][:, 0:CHA],
                     waits=[ev_mbc] + W(ev_t1[p_]), inc=True)
            ev_t1[p_] = vop(c, "tensor_mul",
                            c.xln[:, j * T + t0: j * T + t0 + CHA],
                            c.t1[p_][:, 0:CHA], c.pb[3][:, 0:CHA],
                            waits=[et, ev_rbc], inc=True)
            last = ev_t1[p_]
        ev_apply[ch] = last
        ev_ln_bc_user = last

    ev_evac = [None, None]
    ev_kstage = [None, None]
    ev_wres_user = [None, None]
    tile_ct = 0

    def proj_tile(scr, r0, c0, lhs_fn, rhs_fn, wload):
        nonlocal tile_ct
        bank = tile_ct % 2
        tile_ct += 1
        ev = None
        for i in range(ND):
            w = (list(wload) + W(ev_evac[bank])) if i == 0 else []
            ev = mm(c, c.pb[bank][:], lhs_fn(i), rhs_fn(i),
                    start=(i == 0), stop=(i == ND - 1), waits=w,
                    inc=(i == ND - 1))
        eva = sop(c, "copy", c.kstage[bank][:], c.pb[bank][:],
                  waits=[ev] + W(ev_kstage[bank]), inc=True)
        ev_evac[bank] = eva
        ev_kstage[bank] = dma(c, "scalar", f"qt{bank}",
                              d[scr].ap()[r0:r0 + 128, c0:c0 + 512],
                              c.kstage[bank][:], waits=[eva])

    wkq = d["wk"].ap().rearrange("(i p) o -> p i o", p=128)
    ev_wl_q0 = dma(c, "gpsimd", "qw0",
                   c.wres[0][:].rearrange("p (i o) -> p i o", i=ND),
                   wkq[:, :, 0:512])
    for cc in range(8):
        emit_ln1_chunk(2 * cc)
        emit_ln1_chunk(2 * cc + 1)
        for o in range(4):
            proj_tile("kscr", o * 128, cc * 512,
                      lambda i, o=o: c.wres[0][:, i * 512 + o * 128:i * 512 + o * 128 + 128],
                      lambda i, cc=cc: c.xln[:, i * T + cc * 512:i * T + cc * 512 + 512],
                      [ev_wl_q0, ev_apply[2 * cc + 1]])
    ev_wres_user[0] = r.mark("pe")

    qctr = 1
    cast_before = {"V": ("gwT", "gwc", "qb1"), "Q": ("uwT", "uwc", "qcw0")}
    for wname, producer in (("wk", "K"), ("wv", "V"), ("wq", "Q")):
        if producer in cast_before:
            sn, dn, sem_ = cast_before[producer]
            ev_cast[sn] = dma(c, "gpsimd", sem_, d[dn].ap(), d[sn].ap())
        for qu in range(4):
            if producer == "K" and qu == 0:
                continue
            wb = qctr % 2
            wl = W(ev_wres_user[wb])
            if qctr == 1:
                wl += W(ev_apply[NCHA - 2]) + list(ev_chunk_rd[0] or [])
            srcw = d[wname].ap().rearrange("(i p) o -> p i o", p=128)
            ev_wl = dma(c, "gpsimd", f"qw{wb}",
                        c.wres[wb][:].rearrange("p (i o) -> p i o", i=ND),
                        srcw[:, :, qu * 512:(qu + 1) * 512], waits=wl)
            if producer == "K":
                for cc in range(8):
                    for o in range(4):
                        proj_tile("kscr", (qu * 4 + o) * 128, cc * 512,
                                  lambda i, o=o: c.wres[wb][:, i * 512 + o * 128:i * 512 + o * 128 + 128],
                                  lambda i, cc=cc: c.xln[:, i * T + cc * 512:i * T + cc * 512 + 512],
                                  [ev_wl])
            elif producer == "V":
                for u in range(NS):
                    proj_tile("vscr", u * 128, qu * 512,
                              lambda i, u=u: c.xln[:, i * T + u * 128:i * T + u * 128 + 128],
                              lambda i: c.wres[wb][:, i * 512:(i + 1) * 512],
                              [ev_wl])
            else:
                for cc in range(4):
                    for o in range(4):
                        proj_tile("qscr", (qu * 4 + o) * 128, cc * 512,
                                  lambda i, o=o: c.wres[wb][:, i * 512 + o * 128:i * 512 + o * 128 + 128],
                                  lambda i, cc=cc: c.xln[:, i * T + cc * 512:i * T + cc * 512 + 512],
                                  [ev_wl])
            ev_wres_user[wb] = r.mark("pe")
            qctr += 1

    ev_cast["dwT"] = dma(c, "gpsimd", "qcw1", d["dwc"].ap(), d["dwT"].ap())
    ev_cast_wo = [ev_cast["wo"]]
    ev_cast_g = [ev_cast["gwT"]]
    ev_cast_u = [ev_cast["uwT"]]
    ev_cast_d = [ev_cast["dwT"]]
    ev_phaseA_pe = r.mark("pe")
    ev_scr_done_l = [ev_kstage[0], ev_kstage[1]]
    if DEBUG_DUMPS:
        dbg_dump(c, "dxln", c.xln[:], [ev_phaseA_pe] + list(ev_ln1_done))
        dbg_dump(c, "dkscr", d["kscr"].ap(), ev_scr_done_l)
        dbg_dump(c, "dvscr", d["vscr"].ap(), ev_scr_done_l)
        dbg_dump(c, "dqscr", d["qscr"].ap(), ev_scr_done_l)

    maskT3 = d["maskT"].ap().rearrange("(u p) t -> p u t", p=128)
    vscr3 = d["vscr"].ap().rearrange("(u p) e -> p u e", p=128)

    ev_kh_user = [None, None]
    ev_vh_user = [None, None]
    ev_qh_user = [None, None]
    ev_stt = {}
    ev_exp_par = [None] * 4
    ev_pt_u = {}
    ev_pv_evac = [None, None]
    ev_rR_user = None
    ev_r3_war = None
    ev_recip = None
    ev_bc_prev = None
    ev_zb_user = None
    ev_zbcopy = None
    ev_aall_war = None
    ev_mask_war = [ev_phaseA_pe]
    ev_wob_user = [None, None]
    ev_gbig_user = [None, None]
    ev_ubig_user = [None, None]
    ev_dbig_user = [None, None]
    ev_ao_war = None
    ev_xcown_war = None
    ev_y_war = [ev_phaseA_pe]
    ev_z_war = None
    ev_out_store = None
    ev_hall_war = None
    ev_prev_ln4 = None
    ln_sq_user = sq_user

    for tb in range(NTB):
        tc0 = tb * 512
        ev_mask = dma(c, "sync", "qm", c.mask[:].rearrange("p (u t) -> p u t", u=NS),
                      maskT3[:, :, tc0:tc0 + 512], waits=W(*ev_mask_war))

        ev_ae_all = []
        pending_tail = None
        for h in range(H):
            bh = h % 2
            pvb = 4 if h % 2 == 0 else 7
            klw = W(ev_kh_user[bh])
            if h == 0 and tb == 0:
                klw += ev_scr_done_l
            if h < 2:
                klw += W(ev_xcown_war)
            ev_kh = dma(c, "sync", f"qk{bh}", c.kh[bh][:],
                        d["kscr"].ap()[h * 128:(h + 1) * 128, :], waits=klw)
            vlw = W(ev_vh_user[bh])
            if h == 0 and tb == 0:
                vlw += ev_scr_done_l
            if h < 2:
                vlw += W(ev_xcown_war)
            ev_vh = None
            for part in range(4):
                ev_vh = dma(c, "sync", f"qv{bh}",
                            c.vh[bh][:, part * 1024:(part + 1) * 1024].rearrange(
                                "p (u e) -> p u e", u=8),
                            vscr3[:, part * 8:(part + 1) * 8, h * 128:(h + 1) * 128],
                            waits=(vlw if part == 0 else ()))
            ev_qh = dma(c, "sync", f"qq{bh}", c.qh[bh][:],
                        d["qscr"].ap()[h * 128:(h + 1) * 128, tc0:tc0 + 512],
                        waits=W(ev_qh_user[bh]))

            comp = (list(range(0, 4 * (tb + 1))) +
                    list(range(16, 16 + 4 * tb + 4)))
            skip = (list(range(4 * (tb + 1), 16)) +
                    list(range(16 + 4 * tb + 4, NS)))
            ncmp = len(comp)
            ev_rsum = None
            if skip:
                for si, u in enumerate(skip):
                    ev_rsum = mm(c, c.pb[3][0:1, 0:128], c.ones_bf[:],
                                 c.vh[bh][:, u * 128:(u + 1) * 128],
                                 start=(si == 0), stop=(si == len(skip) - 1),
                                 waits=([ev_vh] + W(ev_r3_war)) if si == 0 else (),
                                 inc=(si == len(skip) - 1))
                ev_revac = sop(c, "copy", c.r_R[:], c.pb[3][0:1, 0:128],
                               waits=[ev_rsum] + W(ev_rR_user), inc=True)
                ev_r3_war = ev_revac
            ev_exp_u = {}
            ev_pv_stop = ev_z_stop = None
            for uu in range(ncmp + 4):
                if uu == 6 and pending_tail is not None:
                    pending_tail()
                    pending_tail = None
                if uu < ncmp:
                    u = comp[uu]
                    sb_ = uu % 3
                    ev_s = mm(c, c.pb[sb_][:], c.kh[bh][:, u * 128:(u + 1) * 128],
                              c.qh[bh][:], start=True, stop=True,
                              waits=[ev_kh, ev_qh] + W(ev_stt.get(sb_)), inc=True)
                    if u < 4 * tb or (16 <= u < 16 + 4 * tb):
                        ev_e = sop(c, "activation", c.pt[:, u * 512:(u + 1) * 512],
                                   c.pb[sb_][:], AF.Exp, scale=INV_SQRT_T,
                                   waits=[ev_s] + W(ev_pt_u.get(u)), inc=True)
                        ev_stt[sb_] = ev_e
                    else:
                        smb = uu % 4
                        ev_sm = vop(c, "scalar_tensor_tensor", c.sm[smb][:],
                                    c.pb[sb_][:], INV_SQRT_T,
                                    c.mask[:, u * 512:(u + 1) * 512],
                                    ALU.mult, ALU.mult,
                                    waits=[ev_s, ev_mask] + W(ev_exp_par[smb]),
                                    inc=True)
                        ev_stt[sb_] = ev_sm
                        ev_e = sop(c, "activation", c.pt[:, u * 512:(u + 1) * 512],
                                   c.sm[smb][:], AF.Exp,
                                   waits=[ev_sm] + W(ev_pt_u.get(u)), inc=True)
                        ev_exp_par[smb] = ev_e
                    ev_exp_u[u] = ev_e
                if uu >= 4:
                    ui = uu - 4
                    u = comp[ui]
                    w = [ev_exp_u[u], ev_vh]
                    if ui == 0:
                        w += W(ev_pv_evac[h % 2])
                    last_pv = (ui == ncmp - 1) and not skip
                    ev_pv_stop = mm(c, c.pb[pvb][:],
                                    c.vh[bh][:, u * 128:(u + 1) * 128],
                                    c.pt[:, u * 512:(u + 1) * 512],
                                    start=(ui == 0), stop=last_pv, waits=w,
                                    inc=True)
                    ev_pt_u[u] = ev_pv_stop
                    wz = []
                    if ui == 0:
                        wz += W(ev_recip)
                    ev_z_stop = mm(c, c.pb[5][0:1, :], c.ones_bf[:],
                                   c.pt[:, u * 512:(u + 1) * 512],
                                   start=(ui == 0), stop=(ui == ncmp - 1), waits=wz,
                                   inc=(ui == ncmp - 1))
            if skip:
                ev_pv_stop = mm(c, c.pb[pvb][:], c.r_R[:], c.onesrow_bf[:],
                                start=False, stop=True,
                                waits=[ev_revac], inc=True)
                ev_rR_user = ev_pv_stop
            ev_z = ev_z_stop
            ev_kh_user[bh] = ev_z
            ev_vh_user[bh] = ("pe", ev_pv_stop[1])
            ev_qh_user[bh] = ev_z
            if skip:
                ev_zc = vop(c, "tensor_scalar_add", c.r_z, c.pb[5][0:1, :],
                            float(len(skip) * 128),
                            waits=[ev_z] + W(ev_bc_prev), inc=True)
            if skip:
                ev_rc = vop(c, "reciprocal", c.r_z, c.r_z,
                            waits=[ev_zc], inc=True)
            else:
                ev_rc = vop(c, "reciprocal", c.r_z, c.pb[5][0:1, :],
                            waits=[ev_z] + W(ev_bc_prev), inc=True)
            ev_recip = ev_rc
            wbc = [ev_rc] + W(ev_zbcopy)
            if skip:
                wbc.append(ev_revac)
            ev_bc = mm(c, c.pb[6][:], c.onesr_f[:], c.r_z,
                       start=True, stop=True, waits=wbc, inc=True)
            ev_bc_prev = ev_bc
            ev_zbc = sop(c, "copy", c.zb[:], c.pb[6][:],
                         waits=[ev_bc] + W(ev_zb_user), inc=True)
            ev_zbcopy = ev_zbc
            ev_ae = vop(c, "tensor_mul", c.a_all[:, h * 512:(h + 1) * 512],
                        c.pb[pvb][:], c.zb[:],
                        waits=[ev_pv_stop, ev_zbc] + W(ev_aall_war), inc=True)
            ev_pv_evac[h % 2] = ev_ae
            ev_zb_user = ev_ae
            ev_ae_all.append(ev_ae)
        if pending_tail is not None:
            pending_tail()
            pending_tail = None
        ev_b2_pe = r.mark("pe")
        ev_b2_dve = r.mark("dve")
        if DEBUG_DUMPS and tb == 0:
            dbg_dump(c, "daall", c.a_all[:], list(ev_ae_all))
            dbg_dump(c, "dpt", c.pt[:], [ev_b2_pe, r.mark("act")])

        bank_war0 = [ev_stt.get(0), ev_stt.get(1), ev_stt.get(2), ev_r3_war,
                     ev_pv_evac[0], ev_recip, ev_zbcopy, ev_pv_evac[1]]
        ev_wo_evac = [None] * ND
        for oh in range(2):
            for i in range(ND):
                bwo = i % 2
                wl = W(ev_wob_user[bwo])
                if tb == 0 and oh == 0 and i < 2:
                    wl.append(ev_phaseA_pe)
                if tb == 0 and oh == 0 and i < 2:
                    wl += ev_cast_wo
                ev_wol = dma(c, "scalar", f"qo{bwo}", c.wob[bwo][:],
                             d["woc"].ap()[i * 128:(i + 1) * 128,
                                           oh * 1024:(oh + 1) * 1024], waits=wl)
                for j in range(8):
                    w = [ev_wol, ev_ae_all[i]]
                    if i == 0:
                        w += W(bank_war0[j] if oh == 0 else ev_wo_evac[j])
                    e_ = mm(c, c.pb[j][:], c.wob[bwo][:, j * 128:(j + 1) * 128],
                            c.a_all[:, i * 512:(i + 1) * 512],
                            start=(i == 0), stop=(i == ND - 1), waits=w,
                            inc=(j == 7))
                ev_wob_user[bwo] = e_
            ev_oh = r.mark("pe")
            for j in range(8):
                ot = oh * 8 + j
                e = vop(c, "tensor_copy",
                        c.ao_all[:, ot * 512:(ot + 1) * 512],
                        c.pb[j][:], waits=[ev_oh] + W(ev_ao_war), inc=True)
                ev_wo_evac[ot if oh == 0 else j] = e
                if oh == 1:
                    ev_wo_evac[ot] = e
        ev_aall_war = r.mark("pe")
        if DEBUG_DUMPS and tb == 0:
            dbg_dump(c, "dao", c.ao_all[:], [e for e in ev_wo_evac if e])

        xw = [ev_kh_user[0], ev_kh_user[1]]
        xw += W(ev_xcown_war)
        ev_xc = dma(c, "sync", "qxo", c.xc_own[:].rearrange("p (j t) -> p j t", j=ND),
                    xT3[:, :, tc0:tc0 + 512], waits=W(*xw))

        ev_mbc2, ev_rbc2, ev_rows2, ln_sq_user = ln_stats(
            c, [c.ao_all[:, j * 512:(j + 1) * 512] for j in range(ND)], 512,
            sum_bank=4, sq_bank=5, mbc_bank=6, rbc_bank=7,
            tile_waits=[[ev_wo_evac[j]] for j in range(ND)],
            war_banks=W(ev_wo_evac[4], ev_wo_evac[5], ev_wo_evac[12], ev_wo_evac[13]),
            war_bc=W(ev_wo_evac[6], ev_wo_evac[7], ev_wo_evac[14], ev_wo_evac[15],
                     ev_prev_ln4), sq_user_prev=ln_sq_user)
        ev_t1b = [None, None]
        ev_yw = []
        for j in range(ND):
            p_ = j % 2
            et = vop(c, "tensor_sub", c.t1[p_][:],
                     c.ao_all[:, j * 512:(j + 1) * 512], c.pb[6][:],
                     waits=[ev_mbc2] + W(ev_t1b[p_], ev_exp_par[p_]), inc=True)
            et2 = vop(c, "tensor_mul", c.t1[p_][:], c.t1[p_][:], c.pb[7][:],
                      waits=[et, ev_rbc2], inc=True)
            ey = pop(c, "tensor_add", c.y_all[:, j * 512:(j + 1) * 512],
                     c.t1[p_][:], c.xc_own[:, j * 512:(j + 1) * 512],
                     waits=[et2, ev_xc] + (W(*ev_y_war) if j == 0 else []),
                     inc=True)
            ev_t1b[p_] = ey
            ev_yw.append(ey)
        ev_xcown_war = ev_yw[-1]
        ev_y_war = []
        if DEBUG_DUMPS and tb == 0:
            dbg_dump(c, "dy", c.y_all[:], ev_yw)

        ev_mbc3, ev_rbc3, ev_rows3, ln_sq_user = ln_stats(
            c, [c.y_all[:, j * 512:(j + 1) * 512] for j in range(ND)], 512,
            sum_bank=4, sq_bank=5, mbc_bank=6, rbc_bank=7,
            tile_waits=[[ev_yw[j]] for j in range(ND)],
            war_banks=[ev_rows2],
            war_bc=[ev_t1b[0], ev_t1b[1]], sq_user_prev=ln_sq_user)
        ev_ln3_pe = r.mark("pe")
        ev_zw = []
        for j in range(ND):
            p_ = j % 2
            et = vop(c, "tensor_sub", c.t1[p_][:],
                     c.y_all[:, j * 512:(j + 1) * 512], c.pb[6][:],
                     waits=[ev_mbc3] + W(ev_t1b[p_]), inc=True)
            ez = vop(c, "tensor_mul", c.z_all[:, j * 512:(j + 1) * 512],
                     c.t1[p_][:], c.pb[7][:],
                     waits=[et, ev_rbc3] + (W(ev_z_war) if j == 0 else []),
                     inc=True)
            ev_t1b[p_] = ez
            ev_zw.append(ez)
        ev_z_done = ev_zw[-1]
        if DEBUG_DUMPS and tb == 0:
            dbg_dump(c, "dz", c.z_all[:], ev_zw)

        ev_silu = [None] * 4
        ev_hstt = [None] * 4
        ev_bank03_war = [ev_wo_evac[j] for j in range(4)]
        for ig in range(NIG):
            for ii in range(4):
                bg = ii % 2
                gl = W(ev_gbig_user[bg])
                if ig == 0 and ii < 2:
                    gl.append(ev_aall_war)
                if tb == 0 and ig == 0 and ii < 2:
                    gl += ev_cast_g
                ev_gl = dma(c, "sync", f"qg{bg}",
                            c.gbig[bg][:].rearrange("p (jj o) -> p jj o", jj=4),
                            d["gwc"].ap().rearrange("(j p) o -> p j o", p=128)[
                                :, ii * 4:(ii + 1) * 4, ig * 512:(ig + 1) * 512],
                            waits=gl)
                for jd in range(4):
                    i = ii * 4 + jd
                    for j in range(4):
                        w = [ev_gl, ev_zw[i]]
                        if i == 0:
                            w += W(ev_silu[j] if ig > 0 else ev_bank03_war[j])
                        e_ = mm(c, c.pb[j][:],
                                c.gbig[bg][:, jd * 512 + j * 128:jd * 512 + j * 128 + 128],
                                c.z_all[:, i * 512:(i + 1) * 512],
                                start=(i == 0), stop=(i == ND - 1), waits=w,
                                inc=(j == 3))
                ev_gbig_user[bg] = e_
            ev_g = r.mark("pe")
            for j in range(4):
                ev_silu[j] = sop(c, "activation", c.sg[j][:], c.pb[j][:],
                                 AF.Silu,
                                 scale=c.gs2_sb[:, ig * 4 + j: ig * 4 + j + 1],
                                 waits=[ev_g, ev_gs] + W(ev_hstt[j], ev_exp_par[j]),
                                 inc=True)
            for ii in range(4):
                bu = ii % 2
                ul = W(ev_ubig_user[bu])
                if ig == 0 and ii < 2:
                    ul.append(ev_aall_war)
                    ul.append(ev_rows3)
                if tb == 0 and ig == 0 and ii < 2:
                    ul += ev_cast_u
                ev_ul = dma(c, "sync", f"qu{bu}",
                            c.ubig[bu][:].rearrange("p (jj o) -> p jj o", jj=4),
                            d["uwc"].ap().rearrange("(j p) o -> p j o", p=128)[
                                :, ii * 4:(ii + 1) * 4, ig * 512:(ig + 1) * 512],
                            waits=ul)
                for jd in range(4):
                    i = ii * 4 + jd
                    for j in range(4):
                        w = [ev_ul, ev_zw[i]]
                        if i == 0:
                            if ig > 0:
                                w += W(ev_hstt[j])
                            else:
                                w += [ev_rows3] if j < 2 else [ev_zw[-1]]
                        e_ = mm(c, c.pb[4 + j][:],
                                c.ubig[bu][:, jd * 512 + j * 128:jd * 512 + j * 128 + 128],
                                c.z_all[:, i * 512:(i + 1) * 512],
                                start=(i == 0), stop=(i == ND - 1), waits=w,
                                inc=(j == 3))
                ev_ubig_user[bu] = e_
            ev_u = r.mark("pe")
            for j in range(4):
                it = ig * 4 + j
                w = [ev_u, ev_silu[j], ev_us]
                if ig == 0 and j == 0:
                    w += W(ev_hall_war, ev_b2_dve)
                ev_hstt[j] = vop(c, "scalar_tensor_tensor",
                                 c.h_all[:, it * 512:(it + 1) * 512],
                                 c.pb[4 + j][:], c.us2_sb[:, it:it + 1],
                                 c.sg[j][:], ALU.mult, ALU.mult,
                                 waits=w, inc=True)
        ev_z_war = r.mark("pe")
        ev_h_done = ev_hstt[3]
        if DEBUG_DUMPS and tb == 0:
            dbg_dump(c, "dh", c.h_all[:], [ev_hstt[j] for j in range(4)])

        ev_dbank_war = list(ev_hstt)
        ev_f = []
        for og in range(4):
            for ii in range(22):
                bd = ii % 2
                dl = W(ev_dbig_user[bd])
                if og == 0 and ii < 2:
                    dl.append(ev_ln3_pe)
                if tb == 0 and og == 0 and ii < 2:
                    dl += ev_cast_d
                ev_dl = dma(c, "scalar", f"qd{bd}",
                            c.dbig[bd][:].rearrange("p (jj o) -> p jj o", jj=2),
                            d["dwc"].ap().rearrange("(j p) o -> p j o", p=128)[
                                :, ii * 2:(ii + 1) * 2, og * 512:(og + 1) * 512],
                            waits=dl)
                for jd in range(2):
                    i = ii * 2 + jd
                    for j in range(4):
                        w = [ev_dl, ev_h_done]
                        if i == 0:
                            w += W(ev_dbank_war[j])
                        e_ = mm(c, c.pb[j][:],
                                c.dbig[bd][:, jd * 512 + j * 128:jd * 512 + j * 128 + 128],
                                c.h_all[:, i * 512:(i + 1) * 512],
                                start=(i == 0), stop=(i == NI - 1), waits=w,
                                inc=(j == 3))
                ev_dbig_user[bd] = e_
            ev_dg = r.mark("pe")
            for j in range(4):
                ot = og * 4 + j
                w = [ev_dg, ev_ds]
                if og == 0 and j == 0:
                    w += W(ev_ao_war)
                e = vop(c, "tensor_scalar_mul",
                        c.ao_all[:, ot * 512:(ot + 1) * 512], c.pb[j][:],
                        c.ds2_sb[:, ot:ot + 1], waits=w, inc=True)
                ev_dbank_war[j] = e
                ev_f.append(e)
        ev_hall_war = r.mark("pe")
        if DEBUG_DUMPS and tb == 0:
            dbg_dump(c, "df", c.ao_all[:], ev_f)

        ev_mbc4, ev_rbc4, ev_rows4, ln_sq_user = ln_stats(
            c, [c.ao_all[:, j * 512:(j + 1) * 512] for j in range(ND)], 512,
            sum_bank=4, sq_bank=5, mbc_bank=6, rbc_bank=7,
            tile_waits=[[ev_f[j]] for j in range(ND)],
            war_banks=[ev_hstt[0], ev_hstt[1]],
            war_bc=W(ev_hstt[2], ev_hstt[3], ev_zw[-1]),
            sq_user_prev=ln_sq_user)
        ev_t1b = [None, None]
        ev_outw = []
        for j in range(ND):
            p_ = j % 2
            et = vop(c, "tensor_sub", c.t1[p_][:],
                     c.ao_all[:, j * 512:(j + 1) * 512], c.pb[6][:],
                     waits=[ev_mbc4] + W(ev_t1b[p_]), inc=True)
            et2 = vop(c, "tensor_mul", c.t1[p_][:], c.t1[p_][:], c.pb[7][:],
                      waits=[et, ev_rbc4], inc=True)
            eo = pop(c, "tensor_add", c.y_all[:, j * 512:(j + 1) * 512],
                     c.t1[p_][:], c.y_all[:, j * 512:(j + 1) * 512],
                     waits=[et2] + ([ev_xcown_war] if j == 0 else []), inc=True)
            ev_t1b[p_] = eo
            ev_outw.append(eo)
        ev_xcown_war = ev_outw[-1]
        ev_y_war = [ev_out_store]
        ev_ao_war = ev_outw[-1]
        for u_ in list(ev_pt_u):
            ev_pt_u[u_] = None
        ev_pt_u[0] = ev_outw[-1]
        ev_pt_u[1] = ev_outw[-1]
        for k_ in range(NS):
            ev_pt_u[k_] = ev_outw[-1]

        ev_out_store = dma(
            c, "sync", "qout",
            d["outT"].ap().rearrange("(j p) t -> p j t", p=128)[:, :, tc0:tc0 + 512],
            c.y_all[:].rearrange("p (j t) -> p j t", j=ND), waits=ev_outw)
        ev_mask_war = [ev_hall_war, r.mark("dve")]
        ev_prev_ln4 = ev_outw[-1]
        ev_exp_par = [ev_outw[-1]] * 4
        for j_ in range(4):
            ev_stt[j_] = ev_dbank_war[j_]
        ev_r3_war = ev_dbank_war[3]

    r.wait("sync", r.mark("qout"))
    if DEBUG_DUMPS:
        r.wait("sync", r.mark("qdbg"))



_NC_CACHE = {}


def kernel(L0_attn_x, L0_attn_Wq, L0_attn_Wk, L0_attn_Wv, L0_attn_Wo,
           L0_ffn_gate_w, L0_ffn_up_w, L0_ffn_down_w,
           L0_ffn_gate_s, L0_ffn_up_s, L0_ffn_down_s):
    x = np.asarray(L0_attn_x, dtype=np.float32)
    wq = np.ascontiguousarray(np.asarray(L0_attn_Wq, dtype=np.float32))
    wk = np.ascontiguousarray(np.asarray(L0_attn_Wk, dtype=np.float32))
    wv = np.ascontiguousarray(np.asarray(L0_attn_Wv, dtype=np.float32))
    wo = np.ascontiguousarray(np.asarray(L0_attn_Wo, dtype=np.float32))
    gwT = np.ascontiguousarray(np.asarray(L0_ffn_gate_w, dtype=np.float32).T)
    uwT = np.ascontiguousarray(np.asarray(L0_ffn_up_w, dtype=np.float32).T)
    dwT = np.ascontiguousarray(np.asarray(L0_ffn_down_w, dtype=np.float32).T)
    gs2 = np.ascontiguousarray(np.asarray(L0_ffn_gate_s, dtype=np.float32).reshape(NI, 128).T)
    us2 = np.ascontiguousarray(np.asarray(L0_ffn_up_s, dtype=np.float32).reshape(NI, 128).T)
    ds2 = np.ascontiguousarray(np.asarray(L0_ffn_down_s, dtype=np.float32).reshape(ND, 128).T)

    if "nc" not in _NC_CACHE:
        _NC_CACHE["nc"] = build_nc()
    nc = _NC_CACHE["nc"]

    in_maps = []
    for core in range(8):
        b, th = core // 2, core % 2
        blocks = np.arange(4096).reshape(16, 256)
        perm = np.concatenate([blocks[th::2].ravel(), blocks[1 - th::2].ravel()])
        xTc = np.ascontiguousarray(x[b][perm, :].T)
        maskc = (perm[:, None] <= perm[None, :2048]).astype(np.float32)
        in_maps.append(dict(xT=xTc, maskT=maskc, wq=wq, wk=wk, wv=wv, wo=wo,
                            gwT=gwT, uwT=uwT, dwT=dwT, gs2=gs2, us2=us2,
                            ds2=ds2))

    res = run_bass_kernel_spmd(nc, in_maps, core_ids=list(range(8)))

    out = np.empty((4, 4096, 2048), dtype=np.float32)
    for core in range(8):
        b, th = core // 2, core % 2
        blocks = np.arange(4096).reshape(16, 256)
        own = blocks[th::2].ravel()
        out[b, own, :] = res.results[core]["outT"].T
    return out


if __name__ == "__main__":
    nc = build_nc()
    print("built ok; instructions:", nc.n_instructions())


# version 14
# speedup vs baseline: 1.1233x; 1.1233x over previous
import numpy as np
import concourse.bass as bass
import concourse.mybir as mybir
from concourse.bass_utils import run_bass_kernel_spmd
from contextlib import ExitStack

F32 = mybir.dt.float32
BF16 = mybir.dt.bfloat16
AF = mybir.ActivationFunctionType
ALU = mybir.AluOpType

D = 2048
T = 4096
TOWN = 2048
H = 16
ND = 16
NS = 32
NTB = 4
NI = 44
NIG = 11
CHA = 256
NCHA = 16
INV_SQRT_T = 1.0 / 64.0
EPS = 1e-5

DEBUG_DUMPS = False
DEBUG_SET = None
DEBUG_SERIAL_DMA = False

ENGS = ("sync", "tensor", "vector", "scalar", "gpsimd")
DMAQ = {"qc", "qdbg", "qx0", "qx1", "qw0", "qw1", "qt0", "qt1", "qm",
        "qk0", "qk1", "qv0", "qv1", "qq0", "qq1", "qo0", "qo1",
        "qg0", "qg1", "qu0", "qu1", "qd0", "qd1", "qxo", "qout",
        "qwo0", "qwo1", "qwg0", "qwg1", "qwu0", "qwu1", "qwd0", "qwd1",
        "qb0", "qb1", "qcw0", "qcw1"}
SEMS = ("pe", "dve", "act", "pool") + tuple(sorted(DMAQ))


class Rec:
    def __init__(self):
        self.items = {e: [] for e in ENGS}
        self.cnt = {s: 0 for s in SEMS}
        self.hw = {e: {} for e in ENGS}

    def wait(self, eng, ev):
        if ev is None:
            return
        sem, val = ev
        if val <= 0:
            return
        if self.hw[eng].get(sem, 0) >= val:
            return
        self.hw[eng][sem] = val
        self.items[eng].append(("w", sem, val))

    def op(self, eng, fn, waits=(), inc=None):
        for ev in waits:
            self.wait(eng, ev)
        if DEBUG_SERIAL_DMA and inc in DMAQ:
            for s_ in DMAQ:
                self.wait(eng, (s_, self.cnt[s_]))
        amt = 0
        if inc is not None:
            amt = 16 if inc in DMAQ else 1
            self.cnt[inc] += amt
        self.items[eng].append(("o", fn, inc, amt))
        return (inc, self.cnt[inc]) if inc else None

    def mark(self, sem):
        return (sem, self.cnt[sem])


class Ctx:
    pass


def build_nc():
    nc = bass.Bass("TRN2", target_bir_lowering=False)
    c = Ctx()
    c.nc = nc
    c.r = Rec()

    dts = {}
    for name, shape, dt, kind in [
        ("xT", [D, T], F32, "ExternalInput"),
        ("maskT", [T, TOWN], F32, "ExternalInput"),
        ("wq", [D, D], F32, "ExternalInput"),
        ("wk", [D, D], F32, "ExternalInput"),
        ("wv", [D, D], F32, "ExternalInput"),
        ("wo", [D, D], F32, "ExternalInput"),
        ("gwT", [D, 5632], F32, "ExternalInput"),
        ("uwT", [D, 5632], F32, "ExternalInput"),
        ("dwT", [5632, D], F32, "ExternalInput"),
        ("gs2", [128, NI], F32, "ExternalInput"),
        ("us2", [128, NI], F32, "ExternalInput"),
        ("ds2", [128, ND], F32, "ExternalInput"),
        ("outT", [D, TOWN], F32, "ExternalOutput"),
        ("kscr", [D, T], BF16, "Internal"),
        ("woc", [D, D], BF16, "Internal"),
        ("gwc", [D, 5632], BF16, "Internal"),
        ("uwc", [D, 5632], BF16, "Internal"),
        ("dwc", [5632, D], BF16, "Internal"),
        ("vscr", [T, D], BF16, "Internal"),
        ("qscr", [D, TOWN], BF16, "Internal"),
    ]:
        dts[name] = nc.dram_tensor(name, shape, dt, kind=kind)
    if DEBUG_DUMPS:
        for name, shape, dt in [
            ("dxln", [128, ND * T], BF16),
            ("dkscr", [D, T], BF16),
            ("dvscr", [T, D], BF16),
            ("dqscr", [D, TOWN], BF16),
            ("daall", [128, H * 512], BF16),
            ("dao", [128, ND * 512], F32),
            ("dy", [128, ND * 512], F32),
            ("dz", [128, ND * 512], BF16),
            ("dh", [128, NI * 512], BF16),
            ("df", [128, ND * 512], F32),
            ("dpt", [128, NS * 512], BF16),
            ("dmrow", [1, 512], F32),
            ("drrow", [1, 512], F32),
        ]:
            dts[name] = nc.dram_tensor(name, shape, dt, kind="ExternalOutput")
    c.dram = dts

    base = ((nc.sbuf_base + 63) // 64) * 64
    off = base

    def take(n):
        nonlocal off
        a = off
        off += (n + 63) // 64 * 64
        return a

    o_consts = take(2368)
    o_rows = take(4096)
    o_small = take(6144)
    o_tmp = take(8192)
    o_zb = take(2048)
    o_aall = take(16384)
    o_big = take(131072)
    o_w = take(32768)
    o_r6 = take(8192)
    assert off <= nc.sbuf_top, f"SBUF overflow: {off} > {nc.sbuf_top}"

    A = nc.alloc_sbuf_tensor_at
    c.onesc_f = A("onesc_f", [128, 1], F32, offset=o_consts)
    c.onesr_f = A("onesr_f", [1, 128], F32, offset=o_consts + 64)
    c.ones_bf = A("ones_bf", [128, 1], BF16, offset=o_consts + 576)
    c.gs2_sb = A("gs2_sb", [128, NI], F32, offset=o_consts + 640)
    c.us2_sb = A("us2_sb", [128, NI], F32, offset=o_consts + 832)
    c.ds2_sb = A("ds2_sb", [128, ND], F32, offset=o_consts + 1024)
    c.onesrow_bf = A("onesrow_bf", [1, 512], BF16, offset=o_consts + 1088)
    c.r_R = A("r_R", [1, 128], BF16, offset=o_consts + 2112)
    rows_a = A("rows_a", [128, 512], F32, offset=o_rows)
    rows_b = A("rows_b", [128, 512], F32, offset=o_rows + 2048)
    c.r_m = rows_a[0:1, :]
    c.r_q = rows_a[32:33, :]
    c.r_var = rows_b[32:33, :]
    c.r_rstd = rows_b[0:1, :]
    c.r_z = rows_a[0:1, :]
    c.kstage = [A(f"kstage{i}", [128, 512], BF16, offset=o_small + i * 1024) for i in range(2)]
    c.qh = [A(f"qh{i}", [128, 512], BF16, offset=o_small + i * 1024) for i in range(2)]
    c.sq = [A(f"sq{i}", [128, 512], BF16, offset=o_small + 2048 + i * 1024) for i in range(2)]
    c.xbf = [A(f"xbf{i}", [128, 512], BF16, offset=o_small + 4096 + i * 1024) for i in range(2)]
    c.sm = [A(f"sm{i}", [128, 512], F32, offset=o_tmp + i * 2048) for i in range(4)]
    c.t1 = [A(f"t1{i}", [128, 512], F32, offset=o_tmp + i * 2048) for i in range(2)]
    c.sg = [A(f"sg{i}", [128, 512], F32, offset=o_tmp + i * 2048) for i in range(4)]
    c.zb = A("zb", [128, 512], F32, offset=o_zb)
    c.mbs = A("mbs", [128, 512], F32, offset=o_tmp + 2 * 2048)
    c.rbs = A("rbs", [128, 512], F32, offset=o_tmp + 3 * 2048)
    c.a_all = A("a_all", [128, H * 512], BF16, offset=o_aall)
    c.gbig = [A(f"gbig{i}", [128, 4 * 512], BF16, offset=o_aall + i * 4096) for i in range(2)]
    c.ubig = [A(f"ubig{i}", [128, 4 * 512], BF16, offset=o_aall + 8192 + i * 4096) for i in range(2)]
    c.xln = A("xln", [128, ND * T], BF16, offset=o_big)
    c.mask = A("mask", [128, NS * 512], F32, offset=o_big)
    c.pt = A("pt", [128, NS * 512], BF16, offset=o_big + 65536)
    c.kh = [A(f"kh{i}", [128, T], BF16, offset=o_big + 98304 + i * 8192) for i in range(2)]
    c.vh = [A(f"vh{i}", [128, T], BF16, offset=o_big + 114688 + i * 8192) for i in range(2)]
    c.h_all = A("h_all", [128, NI * 512], BF16, offset=o_big)
    c.z_all = A("z_all", [128, ND * 512], BF16, offset=o_big + 45056)
    c.ao_all = A("ao_all", [128, ND * 512], F32, offset=o_big + 65536)
    c.xc_own = A("xc_own", [128, ND * 512], F32, offset=o_big + 98304)
    c.out_all = A("out_all", [128, ND * 512], F32, offset=o_big + 98304)
    c.wres = [A(f"wres{i}", [128, ND * 512], BF16, offset=o_w + i * 16384) for i in range(2)]
    c.xc = [A(f"xc{i}", [128, ND * CHA], F32, offset=o_w + i * 16384) for i in range(2)]
    c.y_all = A("y_all", [128, ND * 512], F32, offset=o_w)
    c.wob = [A(f"wob{i}", [128, 1024], BF16, offset=o_r6 + i * 2048) for i in range(2)]
    c.dbig = [A(f"dbig{i}", [128, 2 * 512], BF16, offset=o_r6 + 4096 + i * 2048) for i in range(2)]

    c.pb = [nc.alloc_psum_tensor(f"pb{i}", [128, 512], F32) for i in range(8)]

    record_program(c)

    sems = {}
    with ExitStack() as stack:
        for s in SEMS:
            sems[s] = stack.enter_context(nc.semaphore(s))
        with nc.Block() as blk:
            def mk(ename):
                def f(E):
                    for it in c.r.items[ename]:
                        if it[0] == "w":
                            E.wait_ge(sems[it[1]], it[2])
                        else:
                            inst = it[1](E)
                            if it[2] is not None:
                                inst.then_inc(sems[it[2]], it[3])
                return f
            blk.sync(mk("sync"))
            blk.tensor(mk("tensor"))
            blk.vector(mk("vector"))
            blk.scalar(mk("scalar"))
            blk.gpsimd(mk("gpsimd"))
    return nc



def dma(c, eng, sem, out_ap, in_ap, waits=()):
    def fn(E, o=out_ap, i=in_ap):
        return E.dma_start(o, i)
    return c.r.op(eng, fn, waits=waits, inc=sem)


def mm(c, out, lhsT, rhs, start, stop, waits=(), inc=False):
    def fn(E, o=out, l=lhsT, rr=rhs, st=start, sp=stop):
        return E.matmul(o, l, rr, start=st, stop=sp)
    return c.r.op("tensor", fn, waits=waits, inc="pe" if inc else None)


def vop(c, name, *args, waits=(), inc=False, **kw):
    def fn(E, n=name, a=args, k=kw):
        return getattr(E, n)(*a, **k)
    return c.r.op("vector", fn, waits=waits, inc="dve" if inc else None)


def sop(c, name, *args, waits=(), inc=False, **kw):
    def fn(E, n=name, a=args, k=kw):
        return getattr(E, n)(*a, **k)
    return c.r.op("scalar", fn, waits=waits, inc="act" if inc else None)


def pop(c, name, *args, waits=(), inc=False, **kw):
    def fn(E, n=name, a=args, k=kw):
        return getattr(E, n)(*a, **k)
    return c.r.op("gpsimd", fn, waits=waits, inc="pool" if inc else None)


def W(*evs):
    return [e for e in evs if e is not None]



def ln_stats(c, tiles, n, sum_bank, sq_bank, mbc_bank, rbc_bank,
             tile_waits=None, war_banks=(), war_bc=(), inv_n=1.0 / 2048.0,
             sq_user_prev=(None, None)):
    r = c.r
    nt = len(tiles)
    ev_sqmm = list(sq_user_prev)
    evs = evq = None
    for j, tile in enumerate(tiles):
        tw = list(tile_waits[j]) if tile_waits else []
        b = j % 2
        ev_xb_j = pop(c, "tensor_copy", c.xbf[b][:, 0:n], tile,
                      waits=tw + W(ev_sqmm[b]), inc=True)
        evs = mm(c, c.pb[sum_bank][0:1, 0:n], c.ones_bf[:], c.xbf[b][:, 0:n],
                 start=(j == 0), stop=(j == nt - 1),
                 waits=[ev_xb_j] + (list(war_banks) if j == 0 else []),
                 inc=(j == nt - 1))
        ev_sq_j = pop(c, "tensor_mul", c.sq[b][:, 0:n], tile, tile,
                      waits=tw + W(ev_sqmm[b]), inc=True)
        evq = mm(c, c.pb[sq_bank][0:1, 0:n], c.ones_bf[:], c.sq[b][:, 0:n],
                 start=(j == 0), stop=(j == nt - 1),
                 waits=[ev_sq_j] + (list(war_banks) if j == 0 else []),
                 inc=True)
        ev_sqmm[b] = evq
    e1 = vop(c, "tensor_scalar_mul", c.r_m[0:1, 0:n], c.pb[sum_bank][0:1, 0:n],
             inv_n, waits=[evs], inc=True)
    e2 = vop(c, "tensor_scalar_mul", c.r_q[0:1, 0:n], c.pb[sq_bank][0:1, 0:n],
             inv_n, waits=[evq], inc=True)
    e3 = vop(c, "tensor_mul", c.r_var[0:1, 0:n], c.r_m[0:1, 0:n],
             c.r_m[0:1, 0:n], waits=[e1], inc=True)
    e4 = vop(c, "tensor_sub", c.r_var[0:1, 0:n], c.r_q[0:1, 0:n],
             c.r_var[0:1, 0:n], waits=[e2, e3], inc=True)
    e4b = vop(c, "tensor_scalar_add", c.r_var[0:1, 0:n], c.r_var[0:1, 0:n],
              EPS, waits=[e4], inc=True)
    e5 = sop(c, "activation", c.r_rstd[0:1, 0:n], c.r_var[0:1, 0:n], AF.Ln,
             waits=[e4b], inc=True)
    e6 = sop(c, "activation", c.r_rstd[0:1, 0:n], c.r_rstd[0:1, 0:n], AF.Exp,
             scale=-0.5, waits=[e5], inc=True)
    ev_mbc = mm(c, c.pb[mbc_bank][:, 0:n], c.onesr_f[:], c.r_m[0:1, 0:n],
                start=True, stop=True, waits=[e1] + list(war_bc), inc=True)
    ev_rbc = mm(c, c.pb[rbc_bank][:, 0:n], c.onesr_f[:], c.r_rstd[0:1, 0:n],
                start=True, stop=True, waits=[e6] + list(war_bc), inc=True)
    return ev_mbc, ev_rbc, e2, tuple(ev_sqmm)



def dbg_dump(c, name, src_ap, waits):
    if DEBUG_SET is not None and name not in DEBUG_SET:
        return
    ev = dma(c, "sync", "qdbg", c.dram[name].ap(), src_ap, waits=list(waits))
    for e in ("vector", "scalar", "tensor", "gpsimd", "sync"):
        c.r.wait(e, ev)


def record_program(c):
    d = c.dram
    r = c.r

    pop(c, "memset", c.onesc_f[:], 1.0)
    pop(c, "memset", c.onesr_f[:], 1.0)
    pop(c, "memset", c.onesrow_bf[:], 1.0)
    ev_ones = pop(c, "memset", c.ones_bf[:], 1.0, inc=True)
    r.wait("tensor", ev_ones)
    ev_gs = dma(c, "sync", "qc", c.gs2_sb[:], d["gs2"].ap())
    ev_us = dma(c, "sync", "qc", c.us2_sb[:], d["us2"].ap())
    ev_ds = dma(c, "sync", "qc", c.ds2_sb[:], d["ds2"].ap())
    ev_gs = ev_us = ev_ds = ev_ds

    xT3 = d["xT"].ap().rearrange("(j p) t -> p j t", p=128)

    ev_cast = {}
    ev_cast["wo"] = dma(c, "gpsimd", "qb0", d["woc"].ap(), d["wo"].ap())

    ev_apply = [None] * NCHA
    ev_chunk_rd = [None, None]
    ev_t1 = [None, None]
    ev_ln_bc_user = None
    ev_rows_prev = None
    sq_user = (None, None)
    for ch in range(NCHA):
        b = ch % 2
        t0 = ch * CHA
        w = []
        if ch >= 2:
            w += W(ev_apply[ch - 2]) + list(ev_chunk_rd[b] or [])
        ev_x = dma(c, "sync", f"qx{b}", c.xc[b][:].rearrange("p (j t) -> p j t", j=ND),
                   xT3[:, :, t0:t0 + CHA], waits=w)
        tiles = [c.xc[b][:, j * CHA:(j + 1) * CHA] for j in range(ND)]
        ev_mbc, ev_rbc, ev_rows1, sq_user = ln_stats(
            c, tiles, CHA, sum_bank=4, sq_bank=5, mbc_bank=2, rbc_bank=3,
            tile_waits=[[ev_x]] * ND,
            war_banks=W(ev_rows_prev) if ch >= 1 else [],
            war_bc=W(ev_ln_bc_user) + ([ev_ones] if ch == 0 else []),
            sq_user_prev=sq_user)
        ev_rows_prev = ev_rows1
        ev_chunk_rd[b] = [r.mark("pe"), r.mark("act")]
        last = None
        for j in range(ND):
            p_ = j % 2
            et = vop(c, "tensor_sub", c.t1[p_][:, 0:CHA],
                     c.xc[b][:, j * CHA:(j + 1) * CHA], c.pb[2][:, 0:CHA],
                     waits=[ev_mbc] + W(ev_t1[p_]), inc=True)
            ev_t1[p_] = vop(c, "tensor_mul",
                            c.xln[:, j * T + t0: j * T + t0 + CHA],
                            c.t1[p_][:, 0:CHA], c.pb[3][:, 0:CHA],
                            waits=[et, ev_rbc], inc=True)
            last = ev_t1[p_]
        ev_apply[ch] = last
        ev_ln_bc_user = last
    ev_ln1_done = [ev_apply[-1], r.mark("pe"), r.mark("act")]

    ev_evac = [None, None]
    ev_kstage = [None, None]
    ev_wres_user = [None, None]
    tile_ct = 0

    def proj_tile(scr, r0, c0, lhs_fn, rhs_fn, wload):
        nonlocal tile_ct
        bank = tile_ct % 2
        tile_ct += 1
        ev = None
        for i in range(ND):
            w = ([wload] + W(ev_evac[bank])) if i == 0 else []
            ev = mm(c, c.pb[bank][:], lhs_fn(i), rhs_fn(i),
                    start=(i == 0), stop=(i == ND - 1), waits=w,
                    inc=(i == ND - 1))
        eva = sop(c, "copy", c.kstage[bank][:], c.pb[bank][:],
                  waits=[ev] + W(ev_kstage[bank]), inc=True)
        ev_evac[bank] = eva
        ev_kstage[bank] = dma(c, "scalar", f"qt{bank}",
                              d[scr].ap()[r0:r0 + 128, c0:c0 + 512],
                              c.kstage[bank][:], waits=[eva])

    qctr = 0
    cast_before = {"V": ("gwT", "gwc", "qb1"), "Q": ("uwT", "uwc", "qcw0")}
    for wname, producer in (("wk", "K"), ("wv", "V"), ("wq", "Q")):
        if producer in cast_before:
            sn, dn, sem_ = cast_before[producer]
            ev_cast[sn] = dma(c, "gpsimd", sem_, d[dn].ap(), d[sn].ap())
        for qu in range(4):
            wb = qctr % 2
            wl = W(ev_wres_user[wb])
            if qctr == 0:
                wl += list(ev_ln1_done)
            elif qctr == 1:
                wl += list(ev_ln1_done)
            srcw = d[wname].ap().rearrange("(i p) o -> p i o", p=128)
            ev_wl = dma(c, "gpsimd", f"qw{wb}",
                        c.wres[wb][:].rearrange("p (i o) -> p i o", i=ND),
                        srcw[:, :, qu * 512:(qu + 1) * 512], waits=wl)
            if producer == "K":
                for o in range(4):
                    for cc in range(8):
                        proj_tile("kscr", (qu * 4 + o) * 128, cc * 512,
                                  lambda i, o=o: c.wres[wb][:, i * 512 + o * 128:i * 512 + o * 128 + 128],
                                  lambda i, cc=cc: c.xln[:, i * T + cc * 512:i * T + cc * 512 + 512],
                                  ev_wl)
            elif producer == "V":
                for u in range(NS):
                    proj_tile("vscr", u * 128, qu * 512,
                              lambda i, u=u: c.xln[:, i * T + u * 128:i * T + u * 128 + 128],
                              lambda i: c.wres[wb][:, i * 512:(i + 1) * 512],
                              ev_wl)
            else:
                for o in range(4):
                    for cc in range(4):
                        proj_tile("qscr", (qu * 4 + o) * 128, cc * 512,
                                  lambda i, o=o: c.wres[wb][:, i * 512 + o * 128:i * 512 + o * 128 + 128],
                                  lambda i, cc=cc: c.xln[:, i * T + cc * 512:i * T + cc * 512 + 512],
                                  ev_wl)
            ev_wres_user[wb] = r.mark("pe")
            qctr += 1

    ev_cast["dwT"] = dma(c, "gpsimd", "qcw1", d["dwc"].ap(), d["dwT"].ap())
    ev_cast_wo = [ev_cast["wo"]]
    ev_cast_g = [ev_cast["gwT"]]
    ev_cast_u = [ev_cast["uwT"]]
    ev_cast_d = [ev_cast["dwT"]]
    ev_phaseA_pe = r.mark("pe")
    ev_scr_done_l = [ev_kstage[0], ev_kstage[1]]
    if DEBUG_DUMPS:
        dbg_dump(c, "dxln", c.xln[:], [ev_phaseA_pe] + list(ev_ln1_done))
        dbg_dump(c, "dkscr", d["kscr"].ap(), ev_scr_done_l)
        dbg_dump(c, "dvscr", d["vscr"].ap(), ev_scr_done_l)
        dbg_dump(c, "dqscr", d["qscr"].ap(), ev_scr_done_l)

    maskT3 = d["maskT"].ap().rearrange("(u p) t -> p u t", p=128)
    vscr3 = d["vscr"].ap().rearrange("(u p) e -> p u e", p=128)

    ev_kh_user = [None, None]
    ev_vh_user = [None, None]
    ev_qh_user = [None, None]
    ev_stt = {}
    ev_exp_par = [None] * 4
    ev_pt_u = {}
    ev_pv_evac = [None, None]
    ev_rR_user = None
    ev_r3_war = None
    ev_recip = None
    ev_bc_prev = None
    ev_zb_user = None
    ev_zbcopy = None
    ev_aall_war = None
    ev_mask_war = [ev_phaseA_pe]
    ev_wob_user = [None, None]
    ev_gbig_user = [None, None]
    ev_ubig_user = [None, None]
    ev_dbig_user = [None, None]
    ev_ao_war = None
    ev_xcown_war = None
    ev_y_war = [ev_phaseA_pe]
    ev_z_war = None
    ev_out_store = None
    ev_hall_war = None
    ev_prev_ln4 = None
    ln_sq_user = sq_user

    for tb in range(NTB):
        tc0 = tb * 512
        ev_mask = dma(c, "sync", "qm", c.mask[:].rearrange("p (u t) -> p u t", u=NS),
                      maskT3[:, :, tc0:tc0 + 512], waits=W(*ev_mask_war))

        ev_ae_all = []
        pending_tail = None
        for h in range(H):
            bh = h % 2
            pvb = 4 if h % 2 == 0 else 7
            klw = W(ev_kh_user[bh])
            if h == 0 and tb == 0:
                klw += ev_scr_done_l
            if h < 2:
                klw += W(*ev_xcown_war) if isinstance(ev_xcown_war, list) else W(ev_xcown_war)
            ev_kh = dma(c, "sync", f"qk{bh}", c.kh[bh][:],
                        d["kscr"].ap()[h * 128:(h + 1) * 128, :], waits=klw)
            vlw = W(ev_vh_user[bh])
            if h == 0 and tb == 0:
                vlw += ev_scr_done_l
            if h < 2:
                vlw += W(*ev_xcown_war) if isinstance(ev_xcown_war, list) else W(ev_xcown_war)
            ev_vh = None
            for part in range(4):
                ev_vh = dma(c, "sync", f"qv{bh}",
                            c.vh[bh][:, part * 1024:(part + 1) * 1024].rearrange(
                                "p (u e) -> p u e", u=8),
                            vscr3[:, part * 8:(part + 1) * 8, h * 128:(h + 1) * 128],
                            waits=(vlw if part == 0 else ()))
            ev_qh = dma(c, "sync", f"qq{bh}", c.qh[bh][:],
                        d["qscr"].ap()[h * 128:(h + 1) * 128, tc0:tc0 + 512],
                        waits=W(ev_qh_user[bh]))

            comp = (list(range(0, 4 * (tb + 1))) +
                    list(range(16, 16 + 4 * tb + 4)))
            skip = (list(range(4 * (tb + 1), 16)) +
                    list(range(16 + 4 * tb + 4, NS)))
            ncmp = len(comp)
            ev_rsum = None
            if skip:
                for si, u in enumerate(skip):
                    ev_rsum = mm(c, c.pb[3][0:1, 0:128], c.ones_bf[:],
                                 c.vh[bh][:, u * 128:(u + 1) * 128],
                                 start=(si == 0), stop=(si == len(skip) - 1),
                                 waits=([ev_vh] + W(ev_r3_war)) if si == 0 else (),
                                 inc=(si == len(skip) - 1))
                ev_revac = sop(c, "copy", c.r_R[:], c.pb[3][0:1, 0:128],
                               waits=[ev_rsum] + W(ev_rR_user), inc=True)
                ev_r3_war = ev_revac
            ev_exp_u = {}
            ev_pv_stop = ev_z_stop = None
            for uu in range(ncmp + 4):
                if uu == 6 and pending_tail is not None:
                    pending_tail()
                    pending_tail = None
                if uu < ncmp:
                    u = comp[uu]
                    sb_ = uu % 3
                    ev_s = mm(c, c.pb[sb_][:], c.kh[bh][:, u * 128:(u + 1) * 128],
                              c.qh[bh][:], start=True, stop=True,
                              waits=[ev_kh, ev_qh] + W(ev_stt.get(sb_)), inc=True)
                    if u < 4 * tb or (16 <= u < 16 + 4 * tb):
                        ev_e = sop(c, "activation", c.pt[:, u * 512:(u + 1) * 512],
                                   c.pb[sb_][:], AF.Exp, scale=INV_SQRT_T,
                                   waits=[ev_s] + W(ev_pt_u.get(u)), inc=True)
                        ev_stt[sb_] = ev_e
                    else:
                        smb = uu % 4
                        ev_sm = vop(c, "scalar_tensor_tensor", c.sm[smb][:],
                                    c.pb[sb_][:], INV_SQRT_T,
                                    c.mask[:, u * 512:(u + 1) * 512],
                                    ALU.mult, ALU.mult,
                                    waits=[ev_s, ev_mask] + W(ev_exp_par[smb]),
                                    inc=True)
                        ev_stt[sb_] = ev_sm
                        ev_e = sop(c, "activation", c.pt[:, u * 512:(u + 1) * 512],
                                   c.sm[smb][:], AF.Exp,
                                   waits=[ev_sm] + W(ev_pt_u.get(u)), inc=True)
                        ev_exp_par[smb] = ev_e
                    ev_exp_u[u] = ev_e
                if uu >= 4:
                    ui = uu - 4
                    u = comp[ui]
                    w = [ev_exp_u[u], ev_vh]
                    if ui == 0:
                        w += W(ev_pv_evac[h % 2])
                    last_pv = (ui == ncmp - 1) and not skip
                    ev_pv_stop = mm(c, c.pb[pvb][:],
                                    c.vh[bh][:, u * 128:(u + 1) * 128],
                                    c.pt[:, u * 512:(u + 1) * 512],
                                    start=(ui == 0), stop=last_pv, waits=w,
                                    inc=True)
                    ev_pt_u[u] = ev_pv_stop
                    wz = []
                    if ui == 0:
                        wz += W(ev_recip)
                    ev_z_stop = mm(c, c.pb[5][0:1, :], c.ones_bf[:],
                                   c.pt[:, u * 512:(u + 1) * 512],
                                   start=(ui == 0), stop=(ui == ncmp - 1), waits=wz,
                                   inc=(ui == ncmp - 1))
            if skip:
                ev_pv_stop = mm(c, c.pb[pvb][:], c.r_R[:], c.onesrow_bf[:],
                                start=False, stop=True,
                                waits=[ev_revac], inc=True)
                ev_rR_user = ev_pv_stop
            ev_z = ev_z_stop
            ev_kh_user[bh] = ev_z
            ev_vh_user[bh] = ("pe", ev_pv_stop[1])
            ev_qh_user[bh] = ev_z
            if skip:
                ev_zc = vop(c, "tensor_scalar_add", c.r_z, c.pb[5][0:1, :],
                            float(len(skip) * 128),
                            waits=[ev_z] + W(ev_bc_prev), inc=True)
            if skip:
                ev_rc = vop(c, "reciprocal", c.r_z, c.r_z,
                            waits=[ev_zc], inc=True)
            else:
                ev_rc = vop(c, "reciprocal", c.r_z, c.pb[5][0:1, :],
                            waits=[ev_z] + W(ev_bc_prev), inc=True)
            ev_recip = ev_rc
            wbc = [ev_rc] + W(ev_zbcopy)
            if skip:
                wbc.append(ev_revac)
            ev_bc = mm(c, c.pb[6][:], c.onesr_f[:], c.r_z,
                       start=True, stop=True, waits=wbc, inc=True)
            ev_bc_prev = ev_bc
            ev_zbc = sop(c, "copy", c.zb[:], c.pb[6][:],
                         waits=[ev_bc] + W(ev_zb_user), inc=True)
            ev_zbcopy = ev_zbc
            ev_ae = vop(c, "tensor_mul", c.a_all[:, h * 512:(h + 1) * 512],
                        c.pb[pvb][:], c.zb[:],
                        waits=[ev_pv_stop, ev_zbc] + W(ev_aall_war), inc=True)
            ev_pv_evac[h % 2] = ev_ae
            ev_zb_user = ev_ae
            ev_ae_all.append(ev_ae)
        if pending_tail is not None:
            pending_tail()
            pending_tail = None
        ev_b2_pe = r.mark("pe")
        ev_b2_dve = r.mark("dve")
        if DEBUG_DUMPS and tb == 0:
            dbg_dump(c, "daall", c.a_all[:], list(ev_ae_all))
            dbg_dump(c, "dpt", c.pt[:], [ev_b2_pe, r.mark("act")])

        bank_war0 = [ev_stt.get(0), ev_stt.get(1), ev_stt.get(2), ev_r3_war,
                     ev_pv_evac[0], ev_recip, ev_zbcopy, ev_pv_evac[1]]
        ev_wo_evac = [None] * ND
        for oh in range(2):
            for i in range(ND):
                bwo = i % 2
                wl = W(ev_wob_user[bwo])
                if tb == 0 and oh == 0 and i < 2:
                    wl.append(ev_phaseA_pe)
                if tb == 0 and oh == 0 and i < 2:
                    wl += ev_cast_wo
                ev_wol = dma(c, "scalar", f"qo{bwo}", c.wob[bwo][:],
                             d["woc"].ap()[i * 128:(i + 1) * 128,
                                           oh * 1024:(oh + 1) * 1024], waits=wl)
                for j in range(8):
                    w = [ev_wol, ev_ae_all[i]]
                    if i == 0:
                        w += W(bank_war0[j] if oh == 0 else ev_wo_evac[j])
                    e_ = mm(c, c.pb[j][:], c.wob[bwo][:, j * 128:(j + 1) * 128],
                            c.a_all[:, i * 512:(i + 1) * 512],
                            start=(i == 0), stop=(i == ND - 1), waits=w,
                            inc=(j == 7))
                ev_wob_user[bwo] = e_
            ev_oh = r.mark("pe")
            for j in range(8):
                ot = oh * 8 + j
                e = vop(c, "tensor_copy",
                        c.ao_all[:, ot * 512:(ot + 1) * 512],
                        c.pb[j][:],
                        waits=[ev_oh] + (W(*ev_ao_war) if isinstance(ev_ao_war, list)
                                         else W(ev_ao_war)), inc=True)
                ev_wo_evac[ot if oh == 0 else j] = e
                if oh == 1:
                    ev_wo_evac[ot] = e
        ev_aall_war = r.mark("pe")
        if DEBUG_DUMPS and tb == 0:
            dbg_dump(c, "dao", c.ao_all[:], [e for e in ev_wo_evac if e])

        xw = [ev_kh_user[0], ev_kh_user[1]]
        xw += W(*ev_xcown_war) if isinstance(ev_xcown_war, list) else W(ev_xcown_war)
        ev_xc = dma(c, "sync", "qxo", c.xc_own[:].rearrange("p (j t) -> p j t", j=ND),
                    xT3[:, :, tc0:tc0 + 512], waits=W(*xw))

        ev_mbc2, ev_rbc2, ev_rows2, ln_sq_user = ln_stats(
            c, [c.ao_all[:, j * 512:(j + 1) * 512] for j in range(ND)], 512,
            sum_bank=4, sq_bank=5, mbc_bank=6, rbc_bank=7,
            tile_waits=[[ev_wo_evac[j]] for j in range(ND)],
            war_banks=W(ev_wo_evac[4], ev_wo_evac[5], ev_wo_evac[12], ev_wo_evac[13]),
            war_bc=W(ev_wo_evac[6], ev_wo_evac[7], ev_wo_evac[14], ev_wo_evac[15],
                     ev_prev_ln4), sq_user_prev=ln_sq_user)
        ev_me = sop(c, "copy", c.mbs[:], c.pb[6][:],
                    waits=[ev_mbc2] + W(ev_exp_par[2]), inc=True)
        ev_re = sop(c, "copy", c.rbs[:], c.pb[7][:],
                    waits=[ev_rbc2] + W(ev_exp_par[3]), inc=True)
        ev_t1b = [None, None]
        ev_yw = []
        for j in range(ND):
            p_ = j % 2
            op = vop if j % 2 == 0 else pop
            et = op(c, "tensor_sub", c.t1[p_][:],
                    c.ao_all[:, j * 512:(j + 1) * 512], c.mbs[:],
                    waits=[ev_me] + W(ev_t1b[p_], ev_exp_par[p_]), inc=True)
            et2 = op(c, "tensor_mul", c.t1[p_][:], c.t1[p_][:], c.rbs[:],
                     waits=[et, ev_re], inc=True)
            ey = op(c, "tensor_add", c.y_all[:, j * 512:(j + 1) * 512],
                    c.t1[p_][:], c.xc_own[:, j * 512:(j + 1) * 512],
                    waits=[et2, ev_xc] + (W(*ev_y_war) if j == 0 else []),
                    inc=True)
            ev_t1b[p_] = ey
            ev_yw.append(ey)
        ev_xcown_war = ev_yw[-1]
        ev_y_war = []
        if DEBUG_DUMPS and tb == 0:
            dbg_dump(c, "dy", c.y_all[:], ev_yw)

        ev_mbc3, ev_rbc3, ev_rows3, ln_sq_user = ln_stats(
            c, [c.y_all[:, j * 512:(j + 1) * 512] for j in range(ND)], 512,
            sum_bank=4, sq_bank=5, mbc_bank=6, rbc_bank=7,
            tile_waits=[[ev_yw[j]] for j in range(ND)],
            war_banks=[ev_rows2],
            war_bc=[ev_t1b[0], ev_t1b[1]], sq_user_prev=ln_sq_user)
        ev_ln3_pe = r.mark("pe")
        ev_me3 = sop(c, "copy", c.mbs[:], c.pb[6][:],
                     waits=[ev_mbc3, ev_yw[2]], inc=True)
        ev_re3 = sop(c, "copy", c.rbs[:], c.pb[7][:],
                     waits=[ev_rbc3, ev_yw[3]], inc=True)
        ev_zw = []
        for j in range(ND):
            p_ = j % 2
            op = vop if j % 2 == 0 else pop
            et = op(c, "tensor_sub", c.t1[p_][:],
                    c.y_all[:, j * 512:(j + 1) * 512], c.mbs[:],
                    waits=[ev_me3] + W(ev_t1b[p_]), inc=True)
            ez = op(c, "tensor_mul", c.z_all[:, j * 512:(j + 1) * 512],
                    c.t1[p_][:], c.rbs[:],
                    waits=[et, ev_re3] + (W(ev_z_war) if j == 0 else []),
                    inc=True)
            ev_t1b[p_] = ez
            ev_zw.append(ez)
        ev_z_done = ev_zw[-1]
        if DEBUG_DUMPS and tb == 0:
            dbg_dump(c, "dz", c.z_all[:], ev_zw)

        ev_silu = [None] * 4
        ev_hstt = [None] * 4
        ev_bank03_war = [ev_wo_evac[j] for j in range(4)]
        for ig in range(NIG):
            for ii in range(4):
                bg = ii % 2
                gl = W(ev_gbig_user[bg])
                if ig == 0 and ii < 2:
                    gl.append(ev_aall_war)
                if tb == 0 and ig == 0 and ii < 2:
                    gl += ev_cast_g
                ev_gl = dma(c, "sync", f"qg{bg}",
                            c.gbig[bg][:].rearrange("p (jj o) -> p jj o", jj=4),
                            d["gwc"].ap().rearrange("(j p) o -> p j o", p=128)[
                                :, ii * 4:(ii + 1) * 4, ig * 512:(ig + 1) * 512],
                            waits=gl)
                for jd in range(4):
                    i = ii * 4 + jd
                    for j in range(4):
                        w = [ev_gl, ev_zw[i]]
                        if i == 0:
                            w += W(ev_silu[j] if ig > 0 else ev_bank03_war[j])
                        e_ = mm(c, c.pb[j][:],
                                c.gbig[bg][:, jd * 512 + j * 128:jd * 512 + j * 128 + 128],
                                c.z_all[:, i * 512:(i + 1) * 512],
                                start=(i == 0), stop=(i == ND - 1), waits=w,
                                inc=(j == 3))
                ev_gbig_user[bg] = e_
            ev_g = r.mark("pe")
            for j in range(4):
                ev_silu[j] = sop(c, "activation", c.sg[j][:], c.pb[j][:],
                                 AF.Silu,
                                 scale=c.gs2_sb[:, ig * 4 + j: ig * 4 + j + 1],
                                 waits=[ev_g, ev_gs] + W(ev_hstt[j], ev_exp_par[j]),
                                 inc=True)
            for ii in range(4):
                bu = ii % 2
                ul = W(ev_ubig_user[bu])
                if ig == 0 and ii < 2:
                    ul.append(ev_aall_war)
                    ul.append(ev_rows3)
                if tb == 0 and ig == 0 and ii < 2:
                    ul += ev_cast_u
                ev_ul = dma(c, "sync", f"qu{bu}",
                            c.ubig[bu][:].rearrange("p (jj o) -> p jj o", jj=4),
                            d["uwc"].ap().rearrange("(j p) o -> p j o", p=128)[
                                :, ii * 4:(ii + 1) * 4, ig * 512:(ig + 1) * 512],
                            waits=ul)
                for jd in range(4):
                    i = ii * 4 + jd
                    for j in range(4):
                        w = [ev_ul, ev_zw[i]]
                        if i == 0:
                            if ig > 0:
                                w += W(ev_hstt[j])
                            else:
                                w += [ev_rows3] if j < 2 else [ev_zw[-1]]
                        e_ = mm(c, c.pb[4 + j][:],
                                c.ubig[bu][:, jd * 512 + j * 128:jd * 512 + j * 128 + 128],
                                c.z_all[:, i * 512:(i + 1) * 512],
                                start=(i == 0), stop=(i == ND - 1), waits=w,
                                inc=(j == 3))
                ev_ubig_user[bu] = e_
            ev_u = r.mark("pe")
            for j in range(4):
                it = ig * 4 + j
                w = [ev_u, ev_silu[j], ev_us]
                if ig == 0 and j == 0:
                    w += W(ev_hall_war, ev_b2_dve)
                ev_hstt[j] = vop(c, "scalar_tensor_tensor",
                                 c.h_all[:, it * 512:(it + 1) * 512],
                                 c.pb[4 + j][:], c.us2_sb[:, it:it + 1],
                                 c.sg[j][:], ALU.mult, ALU.mult,
                                 waits=w, inc=True)
        ev_z_war = r.mark("pe")
        ev_h_done = ev_hstt[3]
        if DEBUG_DUMPS and tb == 0:
            dbg_dump(c, "dh", c.h_all[:], [ev_hstt[j] for j in range(4)])

        ev_dbank_war = list(ev_hstt)
        ev_f = []
        for og in range(4):
            for ii in range(22):
                bd = ii % 2
                dl = W(ev_dbig_user[bd])
                if og == 0 and ii < 2:
                    dl.append(ev_ln3_pe)
                if tb == 0 and og == 0 and ii < 2:
                    dl += ev_cast_d
                ev_dl = dma(c, "scalar", f"qd{bd}",
                            c.dbig[bd][:].rearrange("p (jj o) -> p jj o", jj=2),
                            d["dwc"].ap().rearrange("(j p) o -> p j o", p=128)[
                                :, ii * 2:(ii + 1) * 2, og * 512:(og + 1) * 512],
                            waits=dl)
                for jd in range(2):
                    i = ii * 2 + jd
                    for j in range(4):
                        w = [ev_dl, ev_h_done]
                        if i == 0:
                            w += W(ev_dbank_war[j])
                        e_ = mm(c, c.pb[j][:],
                                c.dbig[bd][:, jd * 512 + j * 128:jd * 512 + j * 128 + 128],
                                c.h_all[:, i * 512:(i + 1) * 512],
                                start=(i == 0), stop=(i == NI - 1), waits=w,
                                inc=(j == 3))
                ev_dbig_user[bd] = e_
            ev_dg = r.mark("pe")
            for j in range(4):
                ot = og * 4 + j
                w = [ev_dg, ev_ds]
                if og == 0 and j == 0:
                    w += (W(*ev_ao_war) if isinstance(ev_ao_war, list)
                          else W(ev_ao_war))
                e = vop(c, "tensor_scalar_mul",
                        c.ao_all[:, ot * 512:(ot + 1) * 512], c.pb[j][:],
                        c.ds2_sb[:, ot:ot + 1], waits=w, inc=True)
                ev_dbank_war[j] = e
                ev_f.append(e)
        ev_hall_war = r.mark("pe")
        if DEBUG_DUMPS and tb == 0:
            dbg_dump(c, "df", c.ao_all[:], ev_f)

        ev_mbc4, ev_rbc4, ev_rows4, ln_sq_user = ln_stats(
            c, [c.ao_all[:, j * 512:(j + 1) * 512] for j in range(ND)], 512,
            sum_bank=4, sq_bank=5, mbc_bank=6, rbc_bank=7,
            tile_waits=[[ev_f[j]] for j in range(ND)],
            war_banks=[ev_hstt[0], ev_hstt[1]],
            war_bc=W(ev_hstt[2], ev_hstt[3], ev_zw[-1]),
            sq_user_prev=ln_sq_user)
        ev_me4 = sop(c, "copy", c.mbs[:], c.pb[6][:],
                     waits=[ev_mbc4] + W(ev_hstt[2]), inc=True)
        ev_re4 = sop(c, "copy", c.rbs[:], c.pb[7][:],
                     waits=[ev_rbc4] + W(ev_hstt[3]), inc=True)
        ev_t1b = [None, None]
        ev_outw = []
        for j in range(ND):
            p_ = j % 2
            op = vop if j % 2 == 0 else pop
            et = op(c, "tensor_sub", c.t1[p_][:],
                    c.ao_all[:, j * 512:(j + 1) * 512], c.mbs[:],
                    waits=[ev_me4] + W(ev_t1b[p_]), inc=True)
            et2 = op(c, "tensor_mul", c.t1[p_][:], c.t1[p_][:], c.rbs[:],
                     waits=[et] + W(ev_re4), inc=True)
            eo = op(c, "tensor_add", c.y_all[:, j * 512:(j + 1) * 512],
                    c.t1[p_][:], c.y_all[:, j * 512:(j + 1) * 512],
                    waits=[et2], inc=True)
            ev_t1b[p_] = eo
            ev_outw.append(eo)
        ev_xcown_war = ev_outw[-1]
        ev_y_war = [ev_out_store]
        ev_ao_war = ev_outw[-1]
        for u_ in list(ev_pt_u):
            ev_pt_u[u_] = None
        ev_pt_u[0] = ev_outw[-1]
        ev_pt_u[1] = ev_outw[-1]
        for k_ in range(NS):
            ev_pt_u[k_] = ev_outw[-1]

        ev_out_store = dma(
            c, "sync", "qout",
            d["outT"].ap().rearrange("(j p) t -> p j t", p=128)[:, :, tc0:tc0 + 512],
            c.y_all[:].rearrange("p (j t) -> p j t", j=ND), waits=ev_outw)
        ev_mask_war = [ev_hall_war, r.mark("dve"), r.mark("pool")]
        ev_prev_ln4 = ev_outw[-1]
        ev_exp_par = [ev_outw[ND - 2], ev_outw[ND - 1],
                      ev_outw[ND - 2], ev_outw[ND - 1]]
        c.r.wait("vector", ev_outw[ND - 1])
        c.r.wait("scalar", ev_outw[ND - 1])
        for j_ in range(4):
            ev_stt[j_] = ev_dbank_war[j_]
        ev_r3_war = ev_dbank_war[3]

    r.wait("sync", r.mark("qout"))
    if DEBUG_DUMPS:
        r.wait("sync", r.mark("qdbg"))



_NC_CACHE = {}


def kernel(L0_attn_x, L0_attn_Wq, L0_attn_Wk, L0_attn_Wv, L0_attn_Wo,
           L0_ffn_gate_w, L0_ffn_up_w, L0_ffn_down_w,
           L0_ffn_gate_s, L0_ffn_up_s, L0_ffn_down_s):
    x = np.asarray(L0_attn_x, dtype=np.float32)
    wq = np.ascontiguousarray(np.asarray(L0_attn_Wq, dtype=np.float32))
    wk = np.ascontiguousarray(np.asarray(L0_attn_Wk, dtype=np.float32))
    wv = np.ascontiguousarray(np.asarray(L0_attn_Wv, dtype=np.float32))
    wo = np.ascontiguousarray(np.asarray(L0_attn_Wo, dtype=np.float32))
    gwT = np.ascontiguousarray(np.asarray(L0_ffn_gate_w, dtype=np.float32).T)
    uwT = np.ascontiguousarray(np.asarray(L0_ffn_up_w, dtype=np.float32).T)
    dwT = np.ascontiguousarray(np.asarray(L0_ffn_down_w, dtype=np.float32).T)
    gs2 = np.ascontiguousarray(np.asarray(L0_ffn_gate_s, dtype=np.float32).reshape(NI, 128).T)
    us2 = np.ascontiguousarray(np.asarray(L0_ffn_up_s, dtype=np.float32).reshape(NI, 128).T)
    ds2 = np.ascontiguousarray(np.asarray(L0_ffn_down_s, dtype=np.float32).reshape(ND, 128).T)

    if "nc" not in _NC_CACHE:
        _NC_CACHE["nc"] = build_nc()
    nc = _NC_CACHE["nc"]

    in_maps = []
    for core in range(8):
        b, th = core // 2, core % 2
        blocks = np.arange(4096).reshape(16, 256)
        perm = np.concatenate([blocks[th::2].ravel(), blocks[1 - th::2].ravel()])
        xTc = np.ascontiguousarray(x[b][perm, :].T)
        maskc = (perm[:, None] <= perm[None, :2048]).astype(np.float32)
        in_maps.append(dict(xT=xTc, maskT=maskc, wq=wq, wk=wk, wv=wv, wo=wo,
                            gwT=gwT, uwT=uwT, dwT=dwT, gs2=gs2, us2=us2,
                            ds2=ds2))

    res = run_bass_kernel_spmd(nc, in_maps, core_ids=list(range(8)))

    out = np.empty((4, 4096, 2048), dtype=np.float32)
    for core in range(8):
        b, th = core // 2, core % 2
        blocks = np.arange(4096).reshape(16, 256)
        own = blocks[th::2].ravel()
        out[b, own, :] = res.results[core]["outT"].T
    return out


if __name__ == "__main__":
    nc = build_nc()
    print("built ok; instructions:", nc.n_instructions())


# version 15
# speedup vs baseline: 1.1713x; 1.0428x over previous
import numpy as np
import concourse.bass as bass
import concourse.mybir as mybir
from concourse.bass_utils import run_bass_kernel_spmd
from contextlib import ExitStack

F32 = mybir.dt.float32
BF16 = mybir.dt.bfloat16
AF = mybir.ActivationFunctionType
ALU = mybir.AluOpType

D = 2048
T = 4096
TOWN = 2048
H = 16
ND = 16
NS = 32
NTB = 4
NI = 44
NIG = 11
CHA = 256
NCHA = 16
INV_SQRT_T = 1.0 / 64.0
EPS = 1e-5

DEBUG_DUMPS = False
DEBUG_SET = None
DEBUG_SERIAL_DMA = False

ENGS = ("sync", "tensor", "vector", "scalar", "gpsimd")
DMAQ = {"qc", "qdbg", "qx0", "qx1", "qw0", "qw1", "qt0", "qt1", "qm",
        "qk0", "qk1", "qv0", "qv1", "qq0", "qq1", "qo0", "qo1",
        "qg0", "qg1", "qu0", "qu1", "qd0", "qd1", "qxo", "qout",
        "qwo0", "qwo1", "qwg0", "qwg1", "qwu0", "qwu1", "qwd0", "qwd1",
        "qb0", "qb1", "qcw0", "qcw1"}
SEMS = ("pe", "dve", "act", "pool") + tuple(sorted(DMAQ))


class Rec:
    def __init__(self):
        self.items = {e: [] for e in ENGS}
        self.cnt = {s: 0 for s in SEMS}
        self.hw = {e: {} for e in ENGS}

    def wait(self, eng, ev):
        if ev is None:
            return
        sem, val = ev
        if val <= 0:
            return
        if self.hw[eng].get(sem, 0) >= val:
            return
        self.hw[eng][sem] = val
        self.items[eng].append(("w", sem, val))

    def op(self, eng, fn, waits=(), inc=None):
        for ev in waits:
            self.wait(eng, ev)
        if DEBUG_SERIAL_DMA and inc in DMAQ:
            for s_ in DMAQ:
                self.wait(eng, (s_, self.cnt[s_]))
        amt = 0
        if inc is not None:
            amt = 16 if inc in DMAQ else 1
            self.cnt[inc] += amt
        self.items[eng].append(("o", fn, inc, amt))
        return (inc, self.cnt[inc]) if inc else None

    def mark(self, sem):
        return (sem, self.cnt[sem])


class Ctx:
    pass


def build_nc():
    nc = bass.Bass("TRN2", target_bir_lowering=False)
    c = Ctx()
    c.nc = nc
    c.r = Rec()

    dts = {}
    for name, shape, dt, kind in [
        ("xT", [D, T], F32, "ExternalInput"),
        ("maskT", [T, TOWN], F32, "ExternalInput"),
        ("wq", [D, D], F32, "ExternalInput"),
        ("wk", [D, D], F32, "ExternalInput"),
        ("wv", [D, D], F32, "ExternalInput"),
        ("wo", [D, D], F32, "ExternalInput"),
        ("gwT", [D, 5632], F32, "ExternalInput"),
        ("uwT", [D, 5632], F32, "ExternalInput"),
        ("dwT", [5632, D], F32, "ExternalInput"),
        ("gs2", [128, NI], F32, "ExternalInput"),
        ("us2", [128, NI], F32, "ExternalInput"),
        ("ds2", [128, ND], F32, "ExternalInput"),
        ("outT", [D, TOWN], F32, "ExternalOutput"),
        ("kscr", [D, T], BF16, "Internal"),
        ("woc", [D, D], BF16, "Internal"),
        ("gwc", [D, 5632], BF16, "Internal"),
        ("uwc", [D, 5632], BF16, "Internal"),
        ("dwc", [5632, D], BF16, "Internal"),
        ("vscr", [T, D], BF16, "Internal"),
        ("qscr", [D, TOWN], BF16, "Internal"),
    ]:
        dts[name] = nc.dram_tensor(name, shape, dt, kind=kind)
    if DEBUG_DUMPS:
        for name, shape, dt in [
            ("dxln", [128, ND * T], BF16),
            ("dkscr", [D, T], BF16),
            ("dvscr", [T, D], BF16),
            ("dqscr", [D, TOWN], BF16),
            ("daall", [128, H * 512], BF16),
            ("dao", [128, ND * 512], F32),
            ("dy", [128, ND * 512], F32),
            ("dz", [128, ND * 512], BF16),
            ("dh", [128, NI * 512], BF16),
            ("df", [128, ND * 512], F32),
            ("dpt", [128, NS * 512], BF16),
            ("dmrow", [1, 512], F32),
            ("drrow", [1, 512], F32),
        ]:
            dts[name] = nc.dram_tensor(name, shape, dt, kind="ExternalOutput")
    c.dram = dts

    base = ((nc.sbuf_base + 63) // 64) * 64
    off = base

    def take(n):
        nonlocal off
        a = off
        off += (n + 63) // 64 * 64
        return a

    o_consts = take(2368)
    o_rows = take(4096)
    o_small = take(6144)
    o_tmp = take(8192)
    o_zb = take(2048)
    o_aall = take(16384)
    o_big = take(131072)
    o_w = take(32768)
    o_r6 = take(8192)
    assert off <= nc.sbuf_top, f"SBUF overflow: {off} > {nc.sbuf_top}"

    A = nc.alloc_sbuf_tensor_at
    c.onesc_f = A("onesc_f", [128, 1], F32, offset=o_consts)
    c.onesr_f = A("onesr_f", [1, 128], F32, offset=o_consts + 64)
    c.ones_bf = A("ones_bf", [128, 1], BF16, offset=o_consts + 576)
    c.gs2_sb = A("gs2_sb", [128, NI], F32, offset=o_consts + 640)
    c.us2_sb = A("us2_sb", [128, NI], F32, offset=o_consts + 832)
    c.ds2_sb = A("ds2_sb", [128, ND], F32, offset=o_consts + 1024)
    c.onesrow_bf = A("onesrow_bf", [1, 512], BF16, offset=o_consts + 1088)
    c.r_R = A("r_R", [1, 128], BF16, offset=o_consts + 2112)
    rows_a = A("rows_a", [128, 512], F32, offset=o_rows)
    rows_b = A("rows_b", [128, 512], F32, offset=o_rows + 2048)
    c.r_m = rows_a[0:1, :]
    c.r_q = rows_a[32:33, :]
    c.r_var = rows_b[32:33, :]
    c.r_rstd = rows_b[0:1, :]
    c.r_z = rows_a[0:1, :]
    c.kstage = [A(f"kstage{i}", [128, 512], BF16, offset=o_small + i * 1024) for i in range(2)]
    c.qh = [A(f"qh{i}", [128, 512], BF16, offset=o_small + i * 1024) for i in range(2)]
    c.sq = [A(f"sq{i}", [128, 512], BF16, offset=o_small + 2048 + i * 1024) for i in range(2)]
    c.xbf = [A(f"xbf{i}", [128, 512], BF16, offset=o_small + 4096 + i * 1024) for i in range(2)]
    c.sm = [A(f"sm{i}", [128, 512], F32, offset=o_tmp + i * 2048) for i in range(4)]
    c.t1 = [A(f"t1{i}", [128, 512], F32, offset=o_tmp + i * 2048) for i in range(2)]
    c.sg = [A(f"sg{i}", [128, 512], F32, offset=o_tmp + i * 2048) for i in range(4)]
    c.zb = A("zb", [128, 512], F32, offset=o_zb)
    c.mbs = A("mbs", [128, 512], F32, offset=o_tmp + 2 * 2048)
    c.rbs = A("rbs", [128, 512], F32, offset=o_tmp + 3 * 2048)
    c.a_all = A("a_all", [128, H * 512], BF16, offset=o_aall)
    c.gbig = [A(f"gbig{i}", [128, 4 * 512], BF16, offset=o_aall + i * 4096) for i in range(2)]
    c.ubig = [A(f"ubig{i}", [128, 4 * 512], BF16, offset=o_aall + 8192 + i * 4096) for i in range(2)]
    c.xln = A("xln", [128, ND * T], BF16, offset=o_big)
    c.mask = A("mask", [128, NS * 512], F32, offset=o_big)
    c.pt = A("pt", [128, NS * 512], BF16, offset=o_big + 65536)
    c.kh = [A(f"kh{i}", [128, T], BF16, offset=o_big + 98304 + i * 8192) for i in range(2)]
    c.vh = [A(f"vh{i}", [128, T], BF16, offset=o_big + 114688 + i * 8192) for i in range(2)]
    c.h_all = A("h_all", [128, NI * 512], BF16, offset=o_big)
    c.z_all = A("z_all", [128, ND * 512], BF16, offset=o_big + 45056)
    c.ao_all = A("ao_all", [128, ND * 512], F32, offset=o_big + 65536)
    c.xc_own = A("xc_own", [128, ND * 512], F32, offset=o_big + 98304)
    c.out_all = A("out_all", [128, ND * 512], F32, offset=o_big + 98304)
    c.wres = [A(f"wres{i}", [128, ND * 512], BF16, offset=o_w + i * 16384) for i in range(2)]
    c.xc = [A(f"xc{i}", [128, ND * CHA], F32, offset=o_w + i * 16384) for i in range(2)]
    c.y_all = A("y_all", [128, ND * 512], F32, offset=o_w)
    c.wob = [A(f"wob{i}", [128, 1024], BF16, offset=o_r6 + i * 2048) for i in range(2)]
    c.dbig = [A(f"dbig{i}", [128, 2 * 512], BF16, offset=o_r6 + 4096 + i * 2048) for i in range(2)]

    c.pb = [nc.alloc_psum_tensor(f"pb{i}", [128, 512], F32) for i in range(8)]

    record_program(c)

    sems = {}
    with ExitStack() as stack:
        for s in SEMS:
            sems[s] = stack.enter_context(nc.semaphore(s))
        with nc.Block() as blk:
            def mk(ename):
                def f(E):
                    for it in c.r.items[ename]:
                        if it[0] == "w":
                            E.wait_ge(sems[it[1]], it[2])
                        else:
                            inst = it[1](E)
                            if it[2] is not None:
                                inst.then_inc(sems[it[2]], it[3])
                return f
            blk.sync(mk("sync"))
            blk.tensor(mk("tensor"))
            blk.vector(mk("vector"))
            blk.scalar(mk("scalar"))
            blk.gpsimd(mk("gpsimd"))
    return nc



def dma(c, eng, sem, out_ap, in_ap, waits=()):
    def fn(E, o=out_ap, i=in_ap):
        return E.dma_start(o, i)
    return c.r.op(eng, fn, waits=waits, inc=sem)


def mm(c, out, lhsT, rhs, start, stop, waits=(), inc=False):
    def fn(E, o=out, l=lhsT, rr=rhs, st=start, sp=stop):
        return E.matmul(o, l, rr, start=st, stop=sp)
    return c.r.op("tensor", fn, waits=waits, inc="pe" if inc else None)


def vop(c, name, *args, waits=(), inc=False, **kw):
    def fn(E, n=name, a=args, k=kw):
        return getattr(E, n)(*a, **k)
    return c.r.op("vector", fn, waits=waits, inc="dve" if inc else None)


def sop(c, name, *args, waits=(), inc=False, **kw):
    def fn(E, n=name, a=args, k=kw):
        return getattr(E, n)(*a, **k)
    return c.r.op("scalar", fn, waits=waits, inc="act" if inc else None)


def pop(c, name, *args, waits=(), inc=False, **kw):
    def fn(E, n=name, a=args, k=kw):
        return getattr(E, n)(*a, **k)
    return c.r.op("gpsimd", fn, waits=waits, inc="pool" if inc else None)


def W(*evs):
    return [e for e in evs if e is not None]



def ln_stats(c, tiles, n, sum_bank, sq_bank, mbc_bank, rbc_bank,
             tile_waits=None, war_banks=(), war_bc=(), inv_n=1.0 / 2048.0,
             sq_user_prev=(None, None)):
    r = c.r
    nt = len(tiles)
    ev_sqmm = list(sq_user_prev)
    evs = evq = None
    for j, tile in enumerate(tiles):
        tw = list(tile_waits[j]) if tile_waits else []
        b = j % 2
        ev_xb_j = vop(c, "tensor_copy", c.xbf[b][:, 0:n], tile,
                      waits=tw + W(ev_sqmm[b]), inc=True)
        evs = mm(c, c.pb[sum_bank][0:1, 0:n], c.ones_bf[:], c.xbf[b][:, 0:n],
                 start=(j == 0), stop=(j == nt - 1),
                 waits=[ev_xb_j] + (list(war_banks) if j == 0 else []),
                 inc=(j == nt - 1))
        ev_sq_j = sop(c, "activation", c.sq[b][:, 0:n], tile, AF.Square,
                      waits=tw + W(ev_sqmm[b]), inc=True)
        evq = mm(c, c.pb[sq_bank][0:1, 0:n], c.ones_bf[:], c.sq[b][:, 0:n],
                 start=(j == 0), stop=(j == nt - 1),
                 waits=[ev_sq_j] + (list(war_banks) if j == 0 else []),
                 inc=True)
        ev_sqmm[b] = evq
    e1 = vop(c, "tensor_scalar_mul", c.r_m[0:1, 0:n], c.pb[sum_bank][0:1, 0:n],
             inv_n, waits=[evs], inc=True)
    e2 = vop(c, "tensor_scalar_mul", c.r_q[0:1, 0:n], c.pb[sq_bank][0:1, 0:n],
             inv_n, waits=[evq], inc=True)
    e3 = vop(c, "tensor_mul", c.r_var[0:1, 0:n], c.r_m[0:1, 0:n],
             c.r_m[0:1, 0:n], waits=[e1], inc=True)
    e4 = vop(c, "tensor_sub", c.r_var[0:1, 0:n], c.r_q[0:1, 0:n],
             c.r_var[0:1, 0:n], waits=[e2, e3], inc=True)
    e4b = vop(c, "tensor_scalar_add", c.r_var[0:1, 0:n], c.r_var[0:1, 0:n],
              EPS, waits=[e4], inc=True)
    e5 = sop(c, "activation", c.r_rstd[0:1, 0:n], c.r_var[0:1, 0:n], AF.Ln,
             waits=[e4b], inc=True)
    e6 = sop(c, "activation", c.r_rstd[0:1, 0:n], c.r_rstd[0:1, 0:n], AF.Exp,
             scale=-0.5, waits=[e5], inc=True)
    ev_mbc = mm(c, c.pb[mbc_bank][:, 0:n], c.onesr_f[:], c.r_m[0:1, 0:n],
                start=True, stop=True, waits=[e1] + list(war_bc), inc=True)
    ev_rbc = mm(c, c.pb[rbc_bank][:, 0:n], c.onesr_f[:], c.r_rstd[0:1, 0:n],
                start=True, stop=True, waits=[e6] + list(war_bc), inc=True)
    return ev_mbc, ev_rbc, e2, tuple(ev_sqmm)



def dbg_dump(c, name, src_ap, waits):
    if DEBUG_SET is not None and name not in DEBUG_SET:
        return
    ev = dma(c, "sync", "qdbg", c.dram[name].ap(), src_ap, waits=list(waits))
    for e in ("vector", "scalar", "tensor", "gpsimd", "sync"):
        c.r.wait(e, ev)


def record_program(c):
    d = c.dram
    r = c.r

    pop(c, "memset", c.onesc_f[:], 1.0)
    pop(c, "memset", c.onesr_f[:], 1.0)
    pop(c, "memset", c.onesrow_bf[:], 1.0)
    ev_ones = pop(c, "memset", c.ones_bf[:], 1.0, inc=True)
    r.wait("tensor", ev_ones)
    ev_gs = dma(c, "sync", "qc", c.gs2_sb[:], d["gs2"].ap())
    ev_us = dma(c, "sync", "qc", c.us2_sb[:], d["us2"].ap())
    ev_ds = dma(c, "sync", "qc", c.ds2_sb[:], d["ds2"].ap())
    ev_gs = ev_us = ev_ds = ev_ds

    xT3 = d["xT"].ap().rearrange("(j p) t -> p j t", p=128)

    ev_cast = {}
    ev_cast["wo"] = dma(c, "gpsimd", "qb0", d["woc"].ap(), d["wo"].ap())

    ev_apply = [None] * NCHA
    ev_chunk_rd = [None, None]
    ev_t1 = [None, None]
    ev_ln_bc_user = None
    ev_rows_prev = None
    sq_user = (None, None)
    for ch in range(NCHA):
        b = ch % 2
        t0 = ch * CHA
        w = []
        if ch >= 2:
            w += W(ev_apply[ch - 2]) + list(ev_chunk_rd[b] or [])
        ev_x = dma(c, "sync", f"qx{b}", c.xc[b][:].rearrange("p (j t) -> p j t", j=ND),
                   xT3[:, :, t0:t0 + CHA], waits=w)
        tiles = [c.xc[b][:, j * CHA:(j + 1) * CHA] for j in range(ND)]
        ev_mbc, ev_rbc, ev_rows1, sq_user = ln_stats(
            c, tiles, CHA, sum_bank=4, sq_bank=5, mbc_bank=2, rbc_bank=3,
            tile_waits=[[ev_x]] * ND,
            war_banks=W(ev_rows_prev) if ch >= 1 else [],
            war_bc=W(ev_ln_bc_user) + ([ev_ones] if ch == 0 else []),
            sq_user_prev=sq_user)
        ev_rows_prev = ev_rows1
        ev_chunk_rd[b] = [r.mark("pe"), r.mark("act")]
        last = None
        for j in range(ND):
            p_ = j % 2
            et = vop(c, "tensor_sub", c.t1[p_][:, 0:CHA],
                     c.xc[b][:, j * CHA:(j + 1) * CHA], c.pb[2][:, 0:CHA],
                     waits=[ev_mbc] + W(ev_t1[p_]), inc=True)
            ev_t1[p_] = vop(c, "tensor_mul",
                            c.xln[:, j * T + t0: j * T + t0 + CHA],
                            c.t1[p_][:, 0:CHA], c.pb[3][:, 0:CHA],
                            waits=[et, ev_rbc], inc=True)
            last = ev_t1[p_]
        ev_apply[ch] = last
        ev_ln_bc_user = last
    ev_ln1_done = [ev_apply[-1], r.mark("pe"), r.mark("act")]

    ev_evac = [None, None]
    ev_kstage = [None, None]
    ev_wres_user = [None, None]
    tile_ct = 0

    def proj_tile(scr, r0, c0, lhs_fn, rhs_fn, wload):
        nonlocal tile_ct
        bank = tile_ct % 2
        tile_ct += 1
        ev = None
        for i in range(ND):
            w = ([wload] + W(ev_evac[bank])) if i == 0 else []
            ev = mm(c, c.pb[bank][:], lhs_fn(i), rhs_fn(i),
                    start=(i == 0), stop=(i == ND - 1), waits=w,
                    inc=(i == ND - 1))
        eva = sop(c, "copy", c.kstage[bank][:], c.pb[bank][:],
                  waits=[ev] + W(ev_kstage[bank]), inc=True)
        ev_evac[bank] = eva
        ev_kstage[bank] = dma(c, "scalar", f"qt{bank}",
                              d[scr].ap()[r0:r0 + 128, c0:c0 + 512],
                              c.kstage[bank][:], waits=[eva])

    qctr = 0
    cast_before = {"V": ("gwT", "gwc", "qb1"), "Q": ("uwT", "uwc", "qcw0")}
    for wname, producer in (("wk", "K"), ("wv", "V"), ("wq", "Q")):
        if producer in cast_before:
            sn, dn, sem_ = cast_before[producer]
            ev_cast[sn] = dma(c, "gpsimd", sem_, d[dn].ap(), d[sn].ap())
        for qu in range(4):
            wb = qctr % 2
            wl = W(ev_wres_user[wb])
            if qctr == 0:
                wl += list(ev_ln1_done)
            elif qctr == 1:
                wl += list(ev_ln1_done)
            srcw = d[wname].ap().rearrange("(i p) o -> p i o", p=128)
            ev_wl = dma(c, "gpsimd", f"qw{wb}",
                        c.wres[wb][:].rearrange("p (i o) -> p i o", i=ND),
                        srcw[:, :, qu * 512:(qu + 1) * 512], waits=wl)
            if producer == "K":
                for o in range(4):
                    for cc in range(8):
                        proj_tile("kscr", (qu * 4 + o) * 128, cc * 512,
                                  lambda i, o=o: c.wres[wb][:, i * 512 + o * 128:i * 512 + o * 128 + 128],
                                  lambda i, cc=cc: c.xln[:, i * T + cc * 512:i * T + cc * 512 + 512],
                                  ev_wl)
            elif producer == "V":
                for u in range(NS):
                    proj_tile("vscr", u * 128, qu * 512,
                              lambda i, u=u: c.xln[:, i * T + u * 128:i * T + u * 128 + 128],
                              lambda i: c.wres[wb][:, i * 512:(i + 1) * 512],
                              ev_wl)
            else:
                for o in range(4):
                    for cc in range(4):
                        proj_tile("qscr", (qu * 4 + o) * 128, cc * 512,
                                  lambda i, o=o: c.wres[wb][:, i * 512 + o * 128:i * 512 + o * 128 + 128],
                                  lambda i, cc=cc: c.xln[:, i * T + cc * 512:i * T + cc * 512 + 512],
                                  ev_wl)
            ev_wres_user[wb] = r.mark("pe")
            qctr += 1

    ev_cast["dwT"] = dma(c, "gpsimd", "qcw1", d["dwc"].ap(), d["dwT"].ap())
    ev_cast_wo = [ev_cast["wo"]]
    ev_cast_g = [ev_cast["gwT"]]
    ev_cast_u = [ev_cast["uwT"]]
    ev_cast_d = [ev_cast["dwT"]]
    ev_phaseA_pe = r.mark("pe")
    ev_scr_done_l = [ev_kstage[0], ev_kstage[1]]
    if DEBUG_DUMPS:
        dbg_dump(c, "dxln", c.xln[:], [ev_phaseA_pe] + list(ev_ln1_done))
        dbg_dump(c, "dkscr", d["kscr"].ap(), ev_scr_done_l)
        dbg_dump(c, "dvscr", d["vscr"].ap(), ev_scr_done_l)
        dbg_dump(c, "dqscr", d["qscr"].ap(), ev_scr_done_l)

    maskT3 = d["maskT"].ap().rearrange("(u p) t -> p u t", p=128)
    vscr3 = d["vscr"].ap().rearrange("(u p) e -> p u e", p=128)

    ev_kh_user = [None, None]
    ev_vh_user = [None, None]
    ev_qh_user = [None, None]
    ev_stt = {}
    ev_exp_par = [None] * 4
    ev_pt_u = {}
    ev_pv_evac = [None, None]
    ev_rR_user = None
    ev_r3_war = None
    ev_recip = None
    ev_bc_prev = None
    ev_zb_user = None
    ev_zbcopy = None
    ev_aall_war = None
    ev_mask_war = [ev_phaseA_pe]
    ev_wob_user = [None, None]
    ev_gbig_user = [None, None]
    ev_ubig_user = [None, None]
    ev_dbig_user = [None, None]
    ev_ao_war = None
    ev_xcown_war = None
    ev_y_war = [ev_phaseA_pe]
    ev_z_war = None
    ev_out_store = None
    ev_hall_war = None
    ev_prev_ln4 = None
    ln_sq_user = sq_user

    for tb in range(NTB):
        tc0 = tb * 512
        ev_mask = dma(c, "sync", "qm", c.mask[:].rearrange("p (u t) -> p u t", u=NS),
                      maskT3[:, :, tc0:tc0 + 512], waits=W(*ev_mask_war))

        ev_ae_all = []
        pending_tail = None
        for h in range(H):
            bh = h % 2
            pvb = 4 if h % 2 == 0 else 7
            klw = W(ev_kh_user[bh])
            if h == 0 and tb == 0:
                klw += ev_scr_done_l
            if h < 2:
                klw += W(*ev_xcown_war) if isinstance(ev_xcown_war, list) else W(ev_xcown_war)
            ev_kh = dma(c, "sync", f"qk{bh}", c.kh[bh][:],
                        d["kscr"].ap()[h * 128:(h + 1) * 128, :], waits=klw)
            vlw = W(ev_vh_user[bh])
            if h == 0 and tb == 0:
                vlw += ev_scr_done_l
            if h < 2:
                vlw += W(*ev_xcown_war) if isinstance(ev_xcown_war, list) else W(ev_xcown_war)
            ev_vh = None
            for part in range(4):
                ev_vh = dma(c, "sync", f"qv{bh}",
                            c.vh[bh][:, part * 1024:(part + 1) * 1024].rearrange(
                                "p (u e) -> p u e", u=8),
                            vscr3[:, part * 8:(part + 1) * 8, h * 128:(h + 1) * 128],
                            waits=(vlw if part == 0 else ()))
            ev_qh = dma(c, "sync", f"qq{bh}", c.qh[bh][:],
                        d["qscr"].ap()[h * 128:(h + 1) * 128, tc0:tc0 + 512],
                        waits=W(ev_qh_user[bh]))

            comp = (list(range(0, 4 * (tb + 1))) +
                    list(range(16, 16 + 4 * tb + 4)))
            skip = (list(range(4 * (tb + 1), 16)) +
                    list(range(16 + 4 * tb + 4, NS)))
            ncmp = len(comp)
            ev_rsum = None
            if skip:
                for si, u in enumerate(skip):
                    ev_rsum = mm(c, c.pb[3][0:1, 0:128], c.ones_bf[:],
                                 c.vh[bh][:, u * 128:(u + 1) * 128],
                                 start=(si == 0), stop=(si == len(skip) - 1),
                                 waits=([ev_vh] + W(ev_r3_war)) if si == 0 else (),
                                 inc=(si == len(skip) - 1))
                ev_revac = sop(c, "copy", c.r_R[:], c.pb[3][0:1, 0:128],
                               waits=[ev_rsum] + W(ev_rR_user), inc=True)
                ev_r3_war = ev_revac
            ev_exp_u = {}
            ev_pv_stop = ev_z_stop = None
            for uu in range(ncmp + 4):
                if uu == 6 and pending_tail is not None:
                    pending_tail()
                    pending_tail = None
                if uu < ncmp:
                    u = comp[uu]
                    sb_ = uu % 3
                    ev_s = mm(c, c.pb[sb_][:], c.kh[bh][:, u * 128:(u + 1) * 128],
                              c.qh[bh][:], start=True, stop=True,
                              waits=[ev_kh, ev_qh] + W(ev_stt.get(sb_)), inc=True)
                    if u < 4 * tb or (16 <= u < 16 + 4 * tb):
                        ev_e = sop(c, "activation", c.pt[:, u * 512:(u + 1) * 512],
                                   c.pb[sb_][:], AF.Exp, scale=INV_SQRT_T,
                                   waits=[ev_s] + W(ev_pt_u.get(u)), inc=True)
                        ev_stt[sb_] = ev_e
                    else:
                        smb = uu % 4
                        ev_sm = vop(c, "scalar_tensor_tensor", c.sm[smb][:],
                                    c.pb[sb_][:], INV_SQRT_T,
                                    c.mask[:, u * 512:(u + 1) * 512],
                                    ALU.mult, ALU.mult,
                                    waits=[ev_s, ev_mask] + W(ev_exp_par[smb]),
                                    inc=True)
                        ev_stt[sb_] = ev_sm
                        ev_e = sop(c, "activation", c.pt[:, u * 512:(u + 1) * 512],
                                   c.sm[smb][:], AF.Exp,
                                   waits=[ev_sm] + W(ev_pt_u.get(u)), inc=True)
                        ev_exp_par[smb] = ev_e
                    ev_exp_u[u] = ev_e
                if uu >= 4:
                    ui = uu - 4
                    u = comp[ui]
                    w = [ev_exp_u[u], ev_vh]
                    if ui == 0:
                        w += W(ev_pv_evac[h % 2])
                    last_pv = (ui == ncmp - 1) and not skip
                    ev_pv_stop = mm(c, c.pb[pvb][:],
                                    c.vh[bh][:, u * 128:(u + 1) * 128],
                                    c.pt[:, u * 512:(u + 1) * 512],
                                    start=(ui == 0), stop=last_pv, waits=w,
                                    inc=True)
                    ev_pt_u[u] = ev_pv_stop
                    wz = []
                    if ui == 0:
                        wz += W(ev_recip)
                    ev_z_stop = mm(c, c.pb[5][0:1, :], c.ones_bf[:],
                                   c.pt[:, u * 512:(u + 1) * 512],
                                   start=(ui == 0), stop=(ui == ncmp - 1), waits=wz,
                                   inc=(ui == ncmp - 1))
            if skip:
                ev_pv_stop = mm(c, c.pb[pvb][:], c.r_R[:], c.onesrow_bf[:],
                                start=False, stop=True,
                                waits=[ev_revac], inc=True)
                ev_rR_user = ev_pv_stop
            ev_z = ev_z_stop
            ev_kh_user[bh] = ev_z
            ev_vh_user[bh] = ("pe", ev_pv_stop[1])
            ev_qh_user[bh] = ev_z
            if skip:
                ev_zc = vop(c, "tensor_scalar_add", c.r_z, c.pb[5][0:1, :],
                            float(len(skip) * 128),
                            waits=[ev_z] + W(ev_bc_prev), inc=True)
            if skip:
                ev_rc = vop(c, "reciprocal", c.r_z, c.r_z,
                            waits=[ev_zc], inc=True)
            else:
                ev_rc = vop(c, "reciprocal", c.r_z, c.pb[5][0:1, :],
                            waits=[ev_z] + W(ev_bc_prev), inc=True)
            ev_recip = ev_rc
            wbc = [ev_rc] + W(ev_zbcopy)
            if skip:
                wbc.append(ev_revac)
            ev_bc = mm(c, c.pb[6][:], c.onesr_f[:], c.r_z,
                       start=True, stop=True, waits=wbc, inc=True)
            ev_bc_prev = ev_bc
            ev_zbc = sop(c, "copy", c.zb[:], c.pb[6][:],
                         waits=[ev_bc] + W(ev_zb_user), inc=True)
            ev_zbcopy = ev_zbc
            ev_ae = vop(c, "tensor_mul", c.a_all[:, h * 512:(h + 1) * 512],
                        c.pb[pvb][:], c.zb[:],
                        waits=[ev_pv_stop, ev_zbc] + W(ev_aall_war), inc=True)
            ev_pv_evac[h % 2] = ev_ae
            ev_zb_user = ev_ae
            ev_ae_all.append(ev_ae)
        if pending_tail is not None:
            pending_tail()
            pending_tail = None
        ev_b2_pe = r.mark("pe")
        ev_b2_dve = r.mark("dve")
        if DEBUG_DUMPS and tb == 0:
            dbg_dump(c, "daall", c.a_all[:], list(ev_ae_all))
            dbg_dump(c, "dpt", c.pt[:], [ev_b2_pe, r.mark("act")])

        bank_war0 = [ev_stt.get(0), ev_stt.get(1), ev_stt.get(2), ev_r3_war,
                     ev_pv_evac[0], ev_recip, ev_zbcopy, ev_pv_evac[1]]
        ev_wo_evac = [None] * ND
        for oh in range(2):
            for i in range(ND):
                bwo = i % 2
                wl = W(ev_wob_user[bwo])
                if tb == 0 and oh == 0 and i < 2:
                    wl.append(ev_phaseA_pe)
                if tb == 0 and oh == 0 and i < 2:
                    wl += ev_cast_wo
                ev_wol = dma(c, "scalar", f"qo{bwo}", c.wob[bwo][:],
                             d["woc"].ap()[i * 128:(i + 1) * 128,
                                           oh * 1024:(oh + 1) * 1024], waits=wl)
                for j in range(8):
                    w = [ev_wol, ev_ae_all[i]]
                    if i == 0:
                        w += W(bank_war0[j] if oh == 0 else ev_wo_evac[j])
                    e_ = mm(c, c.pb[j][:], c.wob[bwo][:, j * 128:(j + 1) * 128],
                            c.a_all[:, i * 512:(i + 1) * 512],
                            start=(i == 0), stop=(i == ND - 1), waits=w,
                            inc=(j == 7))
                ev_wob_user[bwo] = e_
            ev_oh = r.mark("pe")
            for j in range(8):
                ot = oh * 8 + j
                e = vop(c, "tensor_copy",
                        c.ao_all[:, ot * 512:(ot + 1) * 512],
                        c.pb[j][:],
                        waits=[ev_oh] + (W(*ev_ao_war) if isinstance(ev_ao_war, list)
                                         else W(ev_ao_war)), inc=True)
                ev_wo_evac[ot if oh == 0 else j] = e
                if oh == 1:
                    ev_wo_evac[ot] = e
        ev_aall_war = r.mark("pe")
        if DEBUG_DUMPS and tb == 0:
            dbg_dump(c, "dao", c.ao_all[:], [e for e in ev_wo_evac if e])

        xw = [ev_kh_user[0], ev_kh_user[1]]
        xw += W(*ev_xcown_war) if isinstance(ev_xcown_war, list) else W(ev_xcown_war)
        ev_xc = dma(c, "sync", "qxo", c.xc_own[:].rearrange("p (j t) -> p j t", j=ND),
                    xT3[:, :, tc0:tc0 + 512], waits=W(*xw))

        ev_mbc2, ev_rbc2, ev_rows2, ln_sq_user = ln_stats(
            c, [c.ao_all[:, j * 512:(j + 1) * 512] for j in range(ND)], 512,
            sum_bank=4, sq_bank=5, mbc_bank=6, rbc_bank=7,
            tile_waits=[[ev_wo_evac[j]] for j in range(ND)],
            war_banks=W(ev_wo_evac[4], ev_wo_evac[5], ev_wo_evac[12], ev_wo_evac[13]),
            war_bc=W(ev_wo_evac[6], ev_wo_evac[7], ev_wo_evac[14], ev_wo_evac[15],
                     ev_prev_ln4), sq_user_prev=ln_sq_user)
        ev_me = sop(c, "copy", c.mbs[:], c.pb[6][:],
                    waits=[ev_mbc2] + W(ev_exp_par[2]), inc=True)
        ev_re = sop(c, "copy", c.rbs[:], c.pb[7][:],
                    waits=[ev_rbc2] + W(ev_exp_par[3]), inc=True)
        ev_t1b = [None, None]
        ev_yw = []
        for j in range(ND):
            p_ = j % 2
            op = pop if j % 4 == 3 else vop
            et = op(c, "tensor_sub", c.t1[p_][:],
                    c.ao_all[:, j * 512:(j + 1) * 512], c.mbs[:],
                    waits=[ev_me] + W(ev_t1b[p_], ev_exp_par[p_]), inc=True)
            et2 = op(c, "tensor_mul", c.t1[p_][:], c.t1[p_][:], c.rbs[:],
                     waits=[et, ev_re], inc=True)
            ey = op(c, "tensor_add", c.y_all[:, j * 512:(j + 1) * 512],
                    c.t1[p_][:], c.xc_own[:, j * 512:(j + 1) * 512],
                    waits=[et2, ev_xc] + (W(*ev_y_war) if j == 0 else []),
                    inc=True)
            ev_t1b[p_] = ey
            ev_yw.append(ey)
        ev_xcown_war = ev_yw[-1]
        ev_y_war = []
        if DEBUG_DUMPS and tb == 0:
            dbg_dump(c, "dy", c.y_all[:], ev_yw)

        ev_mbc3, ev_rbc3, ev_rows3, ln_sq_user = ln_stats(
            c, [c.y_all[:, j * 512:(j + 1) * 512] for j in range(ND)], 512,
            sum_bank=4, sq_bank=5, mbc_bank=6, rbc_bank=7,
            tile_waits=[[ev_yw[j]] for j in range(ND)],
            war_banks=[ev_rows2],
            war_bc=[ev_t1b[0], ev_t1b[1]], sq_user_prev=ln_sq_user)
        ev_ln3_pe = r.mark("pe")
        ev_me3 = sop(c, "copy", c.mbs[:], c.pb[6][:],
                     waits=[ev_mbc3, ev_yw[2]], inc=True)
        ev_re3 = sop(c, "copy", c.rbs[:], c.pb[7][:],
                     waits=[ev_rbc3, ev_yw[3]], inc=True)
        ev_zw = []
        for j in range(ND):
            p_ = j % 2
            op = pop if j % 4 == 3 else vop
            et = op(c, "tensor_sub", c.t1[p_][:],
                    c.y_all[:, j * 512:(j + 1) * 512], c.mbs[:],
                    waits=[ev_me3] + W(ev_t1b[p_]), inc=True)
            ez = op(c, "tensor_mul", c.z_all[:, j * 512:(j + 1) * 512],
                    c.t1[p_][:], c.rbs[:],
                    waits=[et, ev_re3] + (W(ev_z_war) if j == 0 else []),
                    inc=True)
            ev_t1b[p_] = ez
            ev_zw.append(ez)
        ev_z_done = ev_zw[-1]
        if DEBUG_DUMPS and tb == 0:
            dbg_dump(c, "dz", c.z_all[:], ev_zw)

        ev_silu = [None] * 4
        ev_hstt = [None] * 4
        ev_bank03_war = [ev_wo_evac[j] for j in range(4)]
        for ig in range(NIG):
            for ii in range(4):
                bg = ii % 2
                gl = W(ev_gbig_user[bg])
                if ig == 0 and ii < 2:
                    gl.append(ev_aall_war)
                if tb == 0 and ig == 0 and ii < 2:
                    gl += ev_cast_g
                ev_gl = dma(c, "sync", f"qg{bg}",
                            c.gbig[bg][:].rearrange("p (jj o) -> p jj o", jj=4),
                            d["gwc"].ap().rearrange("(j p) o -> p j o", p=128)[
                                :, ii * 4:(ii + 1) * 4, ig * 512:(ig + 1) * 512],
                            waits=gl)
                for jd in range(4):
                    i = ii * 4 + jd
                    for j in range(4):
                        w = [ev_gl, ev_zw[i]]
                        if i == 0:
                            w += W(ev_silu[j] if ig > 0 else ev_bank03_war[j])
                        e_ = mm(c, c.pb[j][:],
                                c.gbig[bg][:, jd * 512 + j * 128:jd * 512 + j * 128 + 128],
                                c.z_all[:, i * 512:(i + 1) * 512],
                                start=(i == 0), stop=(i == ND - 1), waits=w,
                                inc=(j == 3))
                ev_gbig_user[bg] = e_
            ev_g = r.mark("pe")
            for j in range(4):
                ev_silu[j] = sop(c, "activation", c.sg[j][:], c.pb[j][:],
                                 AF.Silu,
                                 scale=c.gs2_sb[:, ig * 4 + j: ig * 4 + j + 1],
                                 waits=[ev_g, ev_gs] + W(ev_hstt[j], ev_exp_par[j]),
                                 inc=True)
            for ii in range(4):
                bu = ii % 2
                ul = W(ev_ubig_user[bu])
                if ig == 0 and ii < 2:
                    ul.append(ev_aall_war)
                    ul.append(ev_rows3)
                if tb == 0 and ig == 0 and ii < 2:
                    ul += ev_cast_u
                ev_ul = dma(c, "sync", f"qu{bu}",
                            c.ubig[bu][:].rearrange("p (jj o) -> p jj o", jj=4),
                            d["uwc"].ap().rearrange("(j p) o -> p j o", p=128)[
                                :, ii * 4:(ii + 1) * 4, ig * 512:(ig + 1) * 512],
                            waits=ul)
                for jd in range(4):
                    i = ii * 4 + jd
                    for j in range(4):
                        w = [ev_ul, ev_zw[i]]
                        if i == 0:
                            if ig > 0:
                                w += W(ev_hstt[j])
                            else:
                                w += [ev_rows3] if j < 2 else [ev_zw[-1]]
                        e_ = mm(c, c.pb[4 + j][:],
                                c.ubig[bu][:, jd * 512 + j * 128:jd * 512 + j * 128 + 128],
                                c.z_all[:, i * 512:(i + 1) * 512],
                                start=(i == 0), stop=(i == ND - 1), waits=w,
                                inc=(j == 3))
                ev_ubig_user[bu] = e_
            ev_u = r.mark("pe")
            for j in range(4):
                it = ig * 4 + j
                w = [ev_u, ev_silu[j], ev_us]
                if ig == 0 and j == 0:
                    w += W(ev_hall_war, ev_b2_dve)
                ev_hstt[j] = vop(c, "scalar_tensor_tensor",
                                 c.h_all[:, it * 512:(it + 1) * 512],
                                 c.pb[4 + j][:], c.us2_sb[:, it:it + 1],
                                 c.sg[j][:], ALU.mult, ALU.mult,
                                 waits=w, inc=True)
        ev_z_war = r.mark("pe")
        ev_h_done = ev_hstt[3]
        if DEBUG_DUMPS and tb == 0:
            dbg_dump(c, "dh", c.h_all[:], [ev_hstt[j] for j in range(4)])

        ev_dbank_war = list(ev_hstt)
        ev_f = []
        for og in range(4):
            for ii in range(22):
                bd = ii % 2
                dl = W(ev_dbig_user[bd])
                if og == 0 and ii < 2:
                    dl.append(ev_ln3_pe)
                if tb == 0 and og == 0 and ii < 2:
                    dl += ev_cast_d
                ev_dl = dma(c, "scalar", f"qd{bd}",
                            c.dbig[bd][:].rearrange("p (jj o) -> p jj o", jj=2),
                            d["dwc"].ap().rearrange("(j p) o -> p j o", p=128)[
                                :, ii * 2:(ii + 1) * 2, og * 512:(og + 1) * 512],
                            waits=dl)
                for jd in range(2):
                    i = ii * 2 + jd
                    for j in range(4):
                        w = [ev_dl, ev_h_done]
                        if i == 0:
                            w += W(ev_dbank_war[j])
                        e_ = mm(c, c.pb[j][:],
                                c.dbig[bd][:, jd * 512 + j * 128:jd * 512 + j * 128 + 128],
                                c.h_all[:, i * 512:(i + 1) * 512],
                                start=(i == 0), stop=(i == NI - 1), waits=w,
                                inc=(j == 3))
                ev_dbig_user[bd] = e_
            ev_dg = r.mark("pe")
            for j in range(4):
                ot = og * 4 + j
                w = [ev_dg, ev_ds]
                if og == 0 and j == 0:
                    w += (W(*ev_ao_war) if isinstance(ev_ao_war, list)
                          else W(ev_ao_war))
                e = vop(c, "tensor_scalar_mul",
                        c.ao_all[:, ot * 512:(ot + 1) * 512], c.pb[j][:],
                        c.ds2_sb[:, ot:ot + 1], waits=w, inc=True)
                ev_dbank_war[j] = e
                ev_f.append(e)
        ev_hall_war = r.mark("pe")
        if DEBUG_DUMPS and tb == 0:
            dbg_dump(c, "df", c.ao_all[:], ev_f)

        ev_mbc4, ev_rbc4, ev_rows4, ln_sq_user = ln_stats(
            c, [c.ao_all[:, j * 512:(j + 1) * 512] for j in range(ND)], 512,
            sum_bank=4, sq_bank=5, mbc_bank=6, rbc_bank=7,
            tile_waits=[[ev_f[j]] for j in range(ND)],
            war_banks=[ev_hstt[0], ev_hstt[1]],
            war_bc=W(ev_hstt[2], ev_hstt[3], ev_zw[-1]),
            sq_user_prev=ln_sq_user)
        ev_me4 = sop(c, "copy", c.mbs[:], c.pb[6][:],
                     waits=[ev_mbc4] + W(ev_hstt[2]), inc=True)
        ev_re4 = sop(c, "copy", c.rbs[:], c.pb[7][:],
                     waits=[ev_rbc4] + W(ev_hstt[3]), inc=True)
        ev_t1b = [None, None]
        ev_outw = []
        for j in range(ND):
            p_ = j % 2
            op = pop if j % 4 == 3 else vop
            et = op(c, "tensor_sub", c.t1[p_][:],
                    c.ao_all[:, j * 512:(j + 1) * 512], c.mbs[:],
                    waits=[ev_me4] + W(ev_t1b[p_]), inc=True)
            et2 = op(c, "tensor_mul", c.t1[p_][:], c.t1[p_][:], c.rbs[:],
                     waits=[et] + W(ev_re4), inc=True)
            eo = op(c, "tensor_add", c.y_all[:, j * 512:(j + 1) * 512],
                    c.t1[p_][:], c.y_all[:, j * 512:(j + 1) * 512],
                    waits=[et2], inc=True)
            ev_t1b[p_] = eo
            ev_outw.append(eo)
        ev_xcown_war = ev_outw[-1]
        ev_y_war = [ev_out_store]
        ev_ao_war = ev_outw[-1]
        for u_ in list(ev_pt_u):
            ev_pt_u[u_] = None
        ev_pt_u[0] = ev_outw[-1]
        ev_pt_u[1] = ev_outw[-1]
        for k_ in range(NS):
            ev_pt_u[k_] = ev_outw[-1]

        ev_out_store = dma(
            c, "sync", "qout",
            d["outT"].ap().rearrange("(j p) t -> p j t", p=128)[:, :, tc0:tc0 + 512],
            c.y_all[:].rearrange("p (j t) -> p j t", j=ND), waits=ev_outw)
        ev_mask_war = [ev_hall_war, r.mark("dve"), r.mark("pool")]
        ev_prev_ln4 = ev_outw[-1]
        ev_exp_par = [ev_outw[ND - 2], ev_outw[ND - 1],
                      ev_outw[ND - 2], ev_outw[ND - 1]]
        c.r.wait("vector", ev_outw[ND - 1])
        c.r.wait("scalar", ev_outw[ND - 1])
        for j_ in range(4):
            ev_stt[j_] = ev_dbank_war[j_]
        ev_r3_war = ev_dbank_war[3]

    r.wait("sync", r.mark("qout"))
    if DEBUG_DUMPS:
        r.wait("sync", r.mark("qdbg"))



_NC_CACHE = {}


def kernel(L0_attn_x, L0_attn_Wq, L0_attn_Wk, L0_attn_Wv, L0_attn_Wo,
           L0_ffn_gate_w, L0_ffn_up_w, L0_ffn_down_w,
           L0_ffn_gate_s, L0_ffn_up_s, L0_ffn_down_s):
    x = np.asarray(L0_attn_x, dtype=np.float32)
    wq = np.ascontiguousarray(np.asarray(L0_attn_Wq, dtype=np.float32))
    wk = np.ascontiguousarray(np.asarray(L0_attn_Wk, dtype=np.float32))
    wv = np.ascontiguousarray(np.asarray(L0_attn_Wv, dtype=np.float32))
    wo = np.ascontiguousarray(np.asarray(L0_attn_Wo, dtype=np.float32))
    gwT = np.ascontiguousarray(np.asarray(L0_ffn_gate_w, dtype=np.float32).T)
    uwT = np.ascontiguousarray(np.asarray(L0_ffn_up_w, dtype=np.float32).T)
    dwT = np.ascontiguousarray(np.asarray(L0_ffn_down_w, dtype=np.float32).T)
    gs2 = np.ascontiguousarray(np.asarray(L0_ffn_gate_s, dtype=np.float32).reshape(NI, 128).T)
    us2 = np.ascontiguousarray(np.asarray(L0_ffn_up_s, dtype=np.float32).reshape(NI, 128).T)
    ds2 = np.ascontiguousarray(np.asarray(L0_ffn_down_s, dtype=np.float32).reshape(ND, 128).T)

    if "nc" not in _NC_CACHE:
        _NC_CACHE["nc"] = build_nc()
    nc = _NC_CACHE["nc"]

    in_maps = []
    for core in range(8):
        b, th = core // 2, core % 2
        blocks = np.arange(4096).reshape(16, 256)
        perm = np.concatenate([blocks[th::2].ravel(), blocks[1 - th::2].ravel()])
        xTc = np.ascontiguousarray(x[b][perm, :].T)
        maskc = (perm[:, None] <= perm[None, :2048]).astype(np.float32)
        in_maps.append(dict(xT=xTc, maskT=maskc, wq=wq, wk=wk, wv=wv, wo=wo,
                            gwT=gwT, uwT=uwT, dwT=dwT, gs2=gs2, us2=us2,
                            ds2=ds2))

    res = run_bass_kernel_spmd(nc, in_maps, core_ids=list(range(8)))

    out = np.empty((4, 4096, 2048), dtype=np.float32)
    for core in range(8):
        b, th = core // 2, core % 2
        blocks = np.arange(4096).reshape(16, 256)
        own = blocks[th::2].ravel()
        out[b, own, :] = res.results[core]["outT"].T
    return out


if __name__ == "__main__":
    nc = build_nc()
    print("built ok; instructions:", nc.n_instructions())


# version 16
# speedup vs baseline: 1.1844x; 1.0112x over previous
import numpy as np
import concourse.bass as bass
import concourse.mybir as mybir
from concourse.bass_utils import run_bass_kernel_spmd
from contextlib import ExitStack

F32 = mybir.dt.float32
BF16 = mybir.dt.bfloat16
AF = mybir.ActivationFunctionType
ALU = mybir.AluOpType

D = 2048
T = 4096
TOWN = 2048
H = 16
ND = 16
NS = 32
NTB = 4
NI = 44
NIG = 11
CHA = 256
NCHA = 16
INV_SQRT_T = 1.0 / 64.0
EPS = 1e-5

DEBUG_DUMPS = False
DEBUG_SET = None
DEBUG_SERIAL_DMA = False

ENGS = ("sync", "tensor", "vector", "scalar", "gpsimd")
DMAQ = {"qc", "qdbg", "qx0", "qx1", "qw0", "qw1", "qt0", "qt1", "qm",
        "qk0", "qk1", "qv0", "qv1", "qq0", "qq1", "qo0", "qo1",
        "qg0", "qg1", "qu0", "qu1", "qd0", "qd1", "qxo", "qout",
        "qwo0", "qwo1", "qwg0", "qwg1", "qwu0", "qwu1", "qwd0", "qwd1",
        "qb0", "qb1", "qcw0", "qcw1"}
SEMS = ("pe", "dve", "act", "pool") + tuple(sorted(DMAQ))


class Rec:
    def __init__(self):
        self.items = {e: [] for e in ENGS}
        self.cnt = {s: 0 for s in SEMS}
        self.hw = {e: {} for e in ENGS}

    def wait(self, eng, ev):
        if ev is None:
            return
        sem, val = ev
        if val <= 0:
            return
        if self.hw[eng].get(sem, 0) >= val:
            return
        self.hw[eng][sem] = val
        self.items[eng].append(("w", sem, val))

    def op(self, eng, fn, waits=(), inc=None):
        for ev in waits:
            self.wait(eng, ev)
        if DEBUG_SERIAL_DMA and inc in DMAQ:
            for s_ in DMAQ:
                self.wait(eng, (s_, self.cnt[s_]))
        amt = 0
        if inc is not None:
            amt = 16 if inc in DMAQ else 1
            self.cnt[inc] += amt
        self.items[eng].append(("o", fn, inc, amt))
        return (inc, self.cnt[inc]) if inc else None

    def mark(self, sem):
        return (sem, self.cnt[sem])


class Ctx:
    pass


def build_nc():
    nc = bass.Bass("TRN2", target_bir_lowering=False)
    c = Ctx()
    c.nc = nc
    c.r = Rec()

    dts = {}
    for name, shape, dt, kind in [
        ("xT", [D, T], F32, "ExternalInput"),
        ("maskT", [T, TOWN], F32, "ExternalInput"),
        ("wq", [D, D], F32, "ExternalInput"),
        ("wk", [D, D], F32, "ExternalInput"),
        ("wv", [D, D], F32, "ExternalInput"),
        ("wo", [D, D], F32, "ExternalInput"),
        ("gwT", [D, 5632], F32, "ExternalInput"),
        ("uwT", [D, 5632], F32, "ExternalInput"),
        ("dwT", [5632, D], F32, "ExternalInput"),
        ("gs2", [128, NI], F32, "ExternalInput"),
        ("us2", [128, NI], F32, "ExternalInput"),
        ("ds2", [128, ND], F32, "ExternalInput"),
        ("outT", [D, TOWN], F32, "ExternalOutput"),
        ("kscr", [D, T], BF16, "Internal"),
        ("woc", [D, D], BF16, "Internal"),
        ("gwc", [D, 5632], BF16, "Internal"),
        ("uwc", [D, 5632], BF16, "Internal"),
        ("dwc", [5632, D], BF16, "Internal"),
        ("vscr", [T, D], BF16, "Internal"),
        ("qscr", [D, TOWN], BF16, "Internal"),
    ]:
        dts[name] = nc.dram_tensor(name, shape, dt, kind=kind)
    if DEBUG_DUMPS:
        for name, shape, dt in [
            ("dxln", [128, ND * T], BF16),
            ("dkscr", [D, T], BF16),
            ("dvscr", [T, D], BF16),
            ("dqscr", [D, TOWN], BF16),
            ("daall", [128, H * 512], BF16),
            ("dao", [128, ND * 512], F32),
            ("dy", [128, ND * 512], F32),
            ("dz", [128, ND * 512], BF16),
            ("dh", [128, NI * 512], BF16),
            ("df", [128, ND * 512], F32),
            ("dpt", [128, NS * 512], BF16),
            ("dmrow", [1, 512], F32),
            ("drrow", [1, 512], F32),
        ]:
            dts[name] = nc.dram_tensor(name, shape, dt, kind="ExternalOutput")
    c.dram = dts

    base = ((nc.sbuf_base + 63) // 64) * 64
    off = base

    def take(n):
        nonlocal off
        a = off
        off += (n + 63) // 64 * 64
        return a

    o_consts = take(2368)
    o_rows = take(4096)
    o_small = take(6144)
    o_tmp = take(8192)
    o_zb = take(2048)
    o_aall = take(16384)
    o_big = take(131072)
    o_w = take(32768)
    o_r6 = take(8192)
    assert off <= nc.sbuf_top, f"SBUF overflow: {off} > {nc.sbuf_top}"

    A = nc.alloc_sbuf_tensor_at
    c.onesc_f = A("onesc_f", [128, 1], F32, offset=o_consts)
    c.onesr_f = A("onesr_f", [1, 128], F32, offset=o_consts + 64)
    c.ones_bf = A("ones_bf", [128, 1], BF16, offset=o_consts + 576)
    c.gs2_sb = A("gs2_sb", [128, NI], F32, offset=o_consts + 640)
    c.us2_sb = A("us2_sb", [128, NI], F32, offset=o_consts + 832)
    c.ds2_sb = A("ds2_sb", [128, ND], F32, offset=o_consts + 1024)
    c.onesrow_bf = A("onesrow_bf", [1, 512], BF16, offset=o_consts + 1088)
    c.r_R = A("r_R", [1, 128], BF16, offset=o_consts + 2112)
    rows_a = A("rows_a", [128, 512], F32, offset=o_rows)
    rows_b = A("rows_b", [128, 512], F32, offset=o_rows + 2048)
    c.r_m = rows_a[0:1, :]
    c.r_q = rows_a[32:33, :]
    c.r_var = rows_b[32:33, :]
    c.r_rstd = rows_b[0:1, :]
    c.r_z = rows_a[0:1, :]
    c.kstage = [A(f"kstage{i}", [128, 512], BF16, offset=o_small + i * 1024) for i in range(2)]
    c.qh = [A(f"qh{i}", [128, 512], BF16, offset=o_small + i * 1024) for i in range(2)]
    c.sq = [A(f"sq{i}", [128, 512], BF16, offset=o_small + 2048 + i * 1024) for i in range(2)]
    c.xbf = [A(f"xbf{i}", [128, 512], BF16, offset=o_small + 4096 + i * 1024) for i in range(2)]
    c.sm = [A(f"sm{i}", [128, 512], F32, offset=o_tmp + i * 2048) for i in range(4)]
    c.t1 = [A(f"t1{i}", [128, 512], F32, offset=o_tmp + i * 2048) for i in range(2)]
    c.sg = [A(f"sg{i}", [128, 512], F32, offset=o_tmp + i * 2048) for i in range(4)]
    c.zb = A("zb", [128, 512], F32, offset=o_zb)
    c.mbs = A("mbs", [128, 512], F32, offset=o_tmp + 2 * 2048)
    c.rbs = A("rbs", [128, 512], F32, offset=o_tmp + 3 * 2048)
    c.a_all = A("a_all", [128, H * 512], BF16, offset=o_aall)
    c.gbig = [A(f"gbig{i}", [128, 4 * 512], BF16, offset=o_aall + i * 4096) for i in range(2)]
    c.ubig = [A(f"ubig{i}", [128, 4 * 512], BF16, offset=o_aall + 8192 + i * 4096) for i in range(2)]
    c.xln = A("xln", [128, ND * T], BF16, offset=o_big)
    c.mask = A("mask", [128, NS * 512], F32, offset=o_big)
    c.pt = A("pt", [128, NS * 512], BF16, offset=o_big + 65536)
    c.kh = [A(f"kh{i}", [128, T], BF16, offset=o_big + 98304 + i * 8192) for i in range(2)]
    c.vh = [A(f"vh{i}", [128, T], BF16, offset=o_big + 114688 + i * 8192) for i in range(2)]
    c.h_all = A("h_all", [128, NI * 512], BF16, offset=o_big)
    c.z_all = A("z_all", [128, ND * 512], BF16, offset=o_big + 45056)
    c.ao_all = A("ao_all", [128, ND * 512], F32, offset=o_big + 65536)
    c.xc_own = A("xc_own", [128, ND * 512], F32, offset=o_big + 98304)
    c.out_all = A("out_all", [128, ND * 512], F32, offset=o_big + 98304)
    c.wres = [A(f"wres{i}", [128, ND * 512], BF16, offset=o_w + i * 16384) for i in range(2)]
    c.xc = [A(f"xc{i}", [128, ND * CHA], F32, offset=o_w + i * 16384) for i in range(2)]
    c.y_all = A("y_all", [128, ND * 512], F32, offset=o_w)
    c.wob = [A(f"wob{i}", [128, 1024], BF16, offset=o_r6 + i * 2048) for i in range(2)]
    c.dbig = [A(f"dbig{i}", [128, 2 * 512], BF16, offset=o_r6 + 4096 + i * 2048) for i in range(2)]

    c.pb = [nc.alloc_psum_tensor(f"pb{i}", [128, 512], F32) for i in range(8)]

    record_program(c)

    sems = {}
    with ExitStack() as stack:
        for s in SEMS:
            sems[s] = stack.enter_context(nc.semaphore(s))
        with nc.Block() as blk:
            def mk(ename):
                def f(E):
                    for it in c.r.items[ename]:
                        if it[0] == "w":
                            E.wait_ge(sems[it[1]], it[2])
                        else:
                            inst = it[1](E)
                            if it[2] is not None:
                                inst.then_inc(sems[it[2]], it[3])
                return f
            blk.sync(mk("sync"))
            blk.tensor(mk("tensor"))
            blk.vector(mk("vector"))
            blk.scalar(mk("scalar"))
            blk.gpsimd(mk("gpsimd"))
    return nc



def dma(c, eng, sem, out_ap, in_ap, waits=()):
    def fn(E, o=out_ap, i=in_ap):
        return E.dma_start(o, i)
    return c.r.op(eng, fn, waits=waits, inc=sem)


def mm(c, out, lhsT, rhs, start, stop, waits=(), inc=False):
    def fn(E, o=out, l=lhsT, rr=rhs, st=start, sp=stop):
        return E.matmul(o, l, rr, start=st, stop=sp)
    return c.r.op("tensor", fn, waits=waits, inc="pe" if inc else None)


def vop(c, name, *args, waits=(), inc=False, **kw):
    def fn(E, n=name, a=args, k=kw):
        return getattr(E, n)(*a, **k)
    return c.r.op("vector", fn, waits=waits, inc="dve" if inc else None)


def sop(c, name, *args, waits=(), inc=False, **kw):
    def fn(E, n=name, a=args, k=kw):
        return getattr(E, n)(*a, **k)
    return c.r.op("scalar", fn, waits=waits, inc="act" if inc else None)


def pop(c, name, *args, waits=(), inc=False, **kw):
    def fn(E, n=name, a=args, k=kw):
        return getattr(E, n)(*a, **k)
    return c.r.op("gpsimd", fn, waits=waits, inc="pool" if inc else None)


def W(*evs):
    return [e for e in evs if e is not None]


def heartbeat(c, ev):
    def fn(E):
        return E.matmul(c.pb[0][0:1, 0:64], c.ones_bf[:], c.pt[:, 0:64],
                        start=True, stop=True)
    return c.r.op("tensor", fn, waits=W(ev))



def ln_stats(c, tiles, n, sum_bank, sq_bank, mbc_bank, rbc_bank,
             tile_waits=None, war_banks=(), war_bc=(), inv_n=1.0 / 2048.0,
             sq_user_prev=(None, None)):
    r = c.r
    nt = len(tiles)
    ev_sqmm = list(sq_user_prev)
    evs = evq = None
    for j, tile in enumerate(tiles):
        tw = list(tile_waits[j]) if tile_waits else []
        b = j % 2
        ev_xb_j = vop(c, "tensor_copy", c.xbf[b][:, 0:n], tile,
                      waits=tw + W(ev_sqmm[b]), inc=True)
        evs = mm(c, c.pb[sum_bank][0:1, 0:n], c.ones_bf[:], c.xbf[b][:, 0:n],
                 start=(j == 0), stop=(j == nt - 1),
                 waits=[ev_xb_j] + (list(war_banks) if j == 0 else []),
                 inc=(j == nt - 1))
        ev_sq_j = sop(c, "activation", c.sq[b][:, 0:n], tile, AF.Square,
                      waits=tw + W(ev_sqmm[b]), inc=True)
        evq = mm(c, c.pb[sq_bank][0:1, 0:n], c.ones_bf[:], c.sq[b][:, 0:n],
                 start=(j == 0), stop=(j == nt - 1),
                 waits=[ev_sq_j] + (list(war_banks) if j == 0 else []),
                 inc=True)
        ev_sqmm[b] = evq
    e1 = vop(c, "tensor_scalar_mul", c.r_m[0:1, 0:n], c.pb[sum_bank][0:1, 0:n],
             inv_n, waits=[evs], inc=True)
    e2 = vop(c, "tensor_scalar_mul", c.r_q[0:1, 0:n], c.pb[sq_bank][0:1, 0:n],
             inv_n, waits=[evq], inc=True)
    e3 = vop(c, "tensor_mul", c.r_var[0:1, 0:n], c.r_m[0:1, 0:n],
             c.r_m[0:1, 0:n], waits=[e1], inc=True)
    e4 = vop(c, "tensor_sub", c.r_var[0:1, 0:n], c.r_q[0:1, 0:n],
             c.r_var[0:1, 0:n], waits=[e2, e3], inc=True)
    e4b = vop(c, "tensor_scalar_add", c.r_var[0:1, 0:n], c.r_var[0:1, 0:n],
              EPS, waits=[e4], inc=True)
    e5 = sop(c, "activation", c.r_rstd[0:1, 0:n], c.r_var[0:1, 0:n], AF.Ln,
             waits=[e4b], inc=True)
    e6 = sop(c, "activation", c.r_rstd[0:1, 0:n], c.r_rstd[0:1, 0:n], AF.Exp,
             scale=-0.5, waits=[e5], inc=True)
    ev_mbc = mm(c, c.pb[mbc_bank][:, 0:n], c.onesr_f[:], c.r_m[0:1, 0:n],
                start=True, stop=True, waits=[e1] + list(war_bc), inc=True)
    ev_rbc = mm(c, c.pb[rbc_bank][:, 0:n], c.onesr_f[:], c.r_rstd[0:1, 0:n],
                start=True, stop=True, waits=[e6] + list(war_bc), inc=True)
    return ev_mbc, ev_rbc, e2, tuple(ev_sqmm)



def dbg_dump(c, name, src_ap, waits):
    if DEBUG_SET is not None and name not in DEBUG_SET:
        return
    ev = dma(c, "sync", "qdbg", c.dram[name].ap(), src_ap, waits=list(waits))
    for e in ("vector", "scalar", "tensor", "gpsimd", "sync"):
        c.r.wait(e, ev)


def record_program(c):
    d = c.dram
    r = c.r

    pop(c, "memset", c.onesc_f[:], 1.0)
    pop(c, "memset", c.onesr_f[:], 1.0)
    pop(c, "memset", c.onesrow_bf[:], 1.0)
    ev_ones = pop(c, "memset", c.ones_bf[:], 1.0, inc=True)
    r.wait("tensor", ev_ones)
    ev_gs = dma(c, "sync", "qc", c.gs2_sb[:], d["gs2"].ap())
    ev_us = dma(c, "sync", "qc", c.us2_sb[:], d["us2"].ap())
    ev_ds = dma(c, "sync", "qc", c.ds2_sb[:], d["ds2"].ap())
    ev_gs = ev_us = ev_ds = ev_ds

    xT3 = d["xT"].ap().rearrange("(j p) t -> p j t", p=128)

    ev_cast = {}
    ev_cast["wo"] = dma(c, "gpsimd", "qb0", d["woc"].ap(), d["wo"].ap())

    ev_apply = [None] * NCHA
    ev_chunk_rd = [None, None]
    ev_t1 = [None, None]
    ev_ln_bc_user = None
    ev_rows_prev = None
    sq_user = (None, None)
    for ch in range(NCHA):
        b = ch % 2
        t0 = ch * CHA
        w = []
        if ch >= 2:
            w += W(ev_apply[ch - 2]) + list(ev_chunk_rd[b] or [])
        ev_x = dma(c, "sync", f"qx{b}", c.xc[b][:].rearrange("p (j t) -> p j t", j=ND),
                   xT3[:, :, t0:t0 + CHA], waits=w)
        tiles = [c.xc[b][:, j * CHA:(j + 1) * CHA] for j in range(ND)]
        ev_mbc, ev_rbc, ev_rows1, sq_user = ln_stats(
            c, tiles, CHA, sum_bank=4, sq_bank=5, mbc_bank=2, rbc_bank=3,
            tile_waits=[[ev_x]] * ND,
            war_banks=W(ev_rows_prev) if ch >= 1 else [],
            war_bc=W(ev_ln_bc_user) + ([ev_ones] if ch == 0 else []),
            sq_user_prev=sq_user)
        ev_rows_prev = ev_rows1
        ev_chunk_rd[b] = [r.mark("pe"), r.mark("act")]
        last = None
        for j in range(ND):
            p_ = j % 2
            et = vop(c, "tensor_sub", c.t1[p_][:, 0:CHA],
                     c.xc[b][:, j * CHA:(j + 1) * CHA], c.pb[2][:, 0:CHA],
                     waits=[ev_mbc] + W(ev_t1[p_]), inc=True)
            ev_t1[p_] = vop(c, "tensor_mul",
                            c.xln[:, j * T + t0: j * T + t0 + CHA],
                            c.t1[p_][:, 0:CHA], c.pb[3][:, 0:CHA],
                            waits=[et, ev_rbc], inc=True)
            last = ev_t1[p_]
            if j % 4 == 3:
                heartbeat(c, last)
        ev_apply[ch] = last
        ev_ln_bc_user = last
    ev_ln1_done = [ev_apply[-1], r.mark("pe"), r.mark("act")]

    ev_evac = [None, None]
    ev_kstage = [None, None]
    ev_wres_user = [None, None]
    tile_ct = 0

    def proj_tile(scr, r0, c0, lhs_fn, rhs_fn, wload):
        nonlocal tile_ct
        bank = tile_ct % 2
        tile_ct += 1
        ev = None
        for i in range(ND):
            w = ([wload] + W(ev_evac[bank])) if i == 0 else []
            ev = mm(c, c.pb[bank][:], lhs_fn(i), rhs_fn(i),
                    start=(i == 0), stop=(i == ND - 1), waits=w,
                    inc=(i == ND - 1))
        eva = sop(c, "copy", c.kstage[bank][:], c.pb[bank][:],
                  waits=[ev] + W(ev_kstage[bank]), inc=True)
        ev_evac[bank] = eva
        ev_kstage[bank] = dma(c, "scalar", f"qt{bank}",
                              d[scr].ap()[r0:r0 + 128, c0:c0 + 512],
                              c.kstage[bank][:], waits=[eva])

    qctr = 0
    cast_before = {"V": ("gwT", "gwc", "qb1"), "Q": ("uwT", "uwc", "qcw0")}
    for wname, producer in (("wk", "K"), ("wv", "V"), ("wq", "Q")):
        if producer in cast_before:
            sn, dn, sem_ = cast_before[producer]
            ev_cast[sn] = dma(c, "gpsimd", sem_, d[dn].ap(), d[sn].ap())
        for qu in range(4):
            wb = qctr % 2
            wl = W(ev_wres_user[wb])
            if qctr == 0:
                wl += list(ev_ln1_done)
            elif qctr == 1:
                wl += list(ev_ln1_done)
            srcw = d[wname].ap().rearrange("(i p) o -> p i o", p=128)
            ev_wl = dma(c, "gpsimd", f"qw{wb}",
                        c.wres[wb][:].rearrange("p (i o) -> p i o", i=ND),
                        srcw[:, :, qu * 512:(qu + 1) * 512], waits=wl)
            if producer == "K":
                for o in range(4):
                    for cc in range(8):
                        proj_tile("kscr", (qu * 4 + o) * 128, cc * 512,
                                  lambda i, o=o: c.wres[wb][:, i * 512 + o * 128:i * 512 + o * 128 + 128],
                                  lambda i, cc=cc: c.xln[:, i * T + cc * 512:i * T + cc * 512 + 512],
                                  ev_wl)
            elif producer == "V":
                for u in range(NS):
                    proj_tile("vscr", u * 128, qu * 512,
                              lambda i, u=u: c.xln[:, i * T + u * 128:i * T + u * 128 + 128],
                              lambda i: c.wres[wb][:, i * 512:(i + 1) * 512],
                              ev_wl)
            else:
                for o in range(4):
                    for cc in range(4):
                        proj_tile("qscr", (qu * 4 + o) * 128, cc * 512,
                                  lambda i, o=o: c.wres[wb][:, i * 512 + o * 128:i * 512 + o * 128 + 128],
                                  lambda i, cc=cc: c.xln[:, i * T + cc * 512:i * T + cc * 512 + 512],
                                  ev_wl)
            ev_wres_user[wb] = r.mark("pe")
            qctr += 1

    ev_cast["dwT"] = dma(c, "gpsimd", "qcw1", d["dwc"].ap(), d["dwT"].ap())
    ev_cast_wo = [ev_cast["wo"]]
    ev_cast_g = [ev_cast["gwT"]]
    ev_cast_u = [ev_cast["uwT"]]
    ev_cast_d = [ev_cast["dwT"]]
    ev_phaseA_pe = r.mark("pe")
    ev_scr_done_l = [ev_kstage[0], ev_kstage[1]]
    if DEBUG_DUMPS:
        dbg_dump(c, "dxln", c.xln[:], [ev_phaseA_pe] + list(ev_ln1_done))
        dbg_dump(c, "dkscr", d["kscr"].ap(), ev_scr_done_l)
        dbg_dump(c, "dvscr", d["vscr"].ap(), ev_scr_done_l)
        dbg_dump(c, "dqscr", d["qscr"].ap(), ev_scr_done_l)

    maskT3 = d["maskT"].ap().rearrange("(u p) t -> p u t", p=128)
    vscr3 = d["vscr"].ap().rearrange("(u p) e -> p u e", p=128)

    ev_kh_user = [None, None]
    ev_vh_user = [None, None]
    ev_qh_user = [None, None]
    ev_stt = {}
    ev_exp_par = [None] * 4
    ev_pt_u = {}
    ev_pv_evac = [None, None]
    ev_rR_user = None
    ev_r3_war = None
    ev_recip = None
    ev_bc_prev = None
    ev_zb_user = None
    ev_zbcopy = None
    ev_aall_war = None
    ev_mask_war = [ev_phaseA_pe]
    ev_wob_user = [None, None]
    ev_gbig_user = [None, None]
    ev_ubig_user = [None, None]
    ev_dbig_user = [None, None]
    ev_ao_war = None
    ev_xcown_war = None
    ev_y_war = [ev_phaseA_pe]
    ev_z_war = None
    ev_out_store = None
    ev_hall_war = None
    ev_prev_ln4 = None
    ln_sq_user = sq_user

    for tb in range(NTB):
        tc0 = tb * 512
        ev_mask = dma(c, "sync", "qm", c.mask[:].rearrange("p (u t) -> p u t", u=NS),
                      maskT3[:, :, tc0:tc0 + 512], waits=W(*ev_mask_war))

        ev_ae_all = []
        pending_tail = None
        for h in range(H):
            bh = h % 2
            pvb = 4 if h % 2 == 0 else 7
            klw = W(ev_kh_user[bh])
            if h == 0 and tb == 0:
                klw += ev_scr_done_l
            if h < 2:
                klw += W(*ev_xcown_war) if isinstance(ev_xcown_war, list) else W(ev_xcown_war)
            ev_kh = dma(c, "sync", f"qk{bh}", c.kh[bh][:],
                        d["kscr"].ap()[h * 128:(h + 1) * 128, :], waits=klw)
            vlw = W(ev_vh_user[bh])
            if h == 0 and tb == 0:
                vlw += ev_scr_done_l
            if h < 2:
                vlw += W(*ev_xcown_war) if isinstance(ev_xcown_war, list) else W(ev_xcown_war)
            ev_vh = None
            for part in range(4):
                ev_vh = dma(c, "sync", f"qv{bh}",
                            c.vh[bh][:, part * 1024:(part + 1) * 1024].rearrange(
                                "p (u e) -> p u e", u=8),
                            vscr3[:, part * 8:(part + 1) * 8, h * 128:(h + 1) * 128],
                            waits=(vlw if part == 0 else ()))
            ev_qh = dma(c, "sync", f"qq{bh}", c.qh[bh][:],
                        d["qscr"].ap()[h * 128:(h + 1) * 128, tc0:tc0 + 512],
                        waits=W(ev_qh_user[bh]))

            comp = (list(range(0, 4 * (tb + 1))) +
                    list(range(16, 16 + 4 * tb + 4)))
            skip = (list(range(4 * (tb + 1), 16)) +
                    list(range(16 + 4 * tb + 4, NS)))
            ncmp = len(comp)
            ev_rsum = None
            if skip:
                for si, u in enumerate(skip):
                    ev_rsum = mm(c, c.pb[3][0:1, 0:128], c.ones_bf[:],
                                 c.vh[bh][:, u * 128:(u + 1) * 128],
                                 start=(si == 0), stop=(si == len(skip) - 1),
                                 waits=([ev_vh] + W(ev_r3_war)) if si == 0 else (),
                                 inc=(si == len(skip) - 1))
                ev_revac = sop(c, "copy", c.r_R[:], c.pb[3][0:1, 0:128],
                               waits=[ev_rsum] + W(ev_rR_user), inc=True)
                ev_r3_war = ev_revac
            ev_exp_u = {}
            ev_pv_stop = ev_z_stop = None
            for uu in range(ncmp + 4):
                if uu == 6 and pending_tail is not None:
                    pending_tail()
                    pending_tail = None
                if uu < ncmp:
                    u = comp[uu]
                    sb_ = uu % 3
                    ev_s = mm(c, c.pb[sb_][:], c.kh[bh][:, u * 128:(u + 1) * 128],
                              c.qh[bh][:], start=True, stop=True,
                              waits=[ev_kh, ev_qh] + W(ev_stt.get(sb_)), inc=True)
                    if u < 4 * tb or (16 <= u < 16 + 4 * tb):
                        ev_e = sop(c, "activation", c.pt[:, u * 512:(u + 1) * 512],
                                   c.pb[sb_][:], AF.Exp, scale=INV_SQRT_T,
                                   waits=[ev_s] + W(ev_pt_u.get(u)), inc=True)
                        ev_stt[sb_] = ev_e
                    else:
                        smb = uu % 4
                        ev_sm = vop(c, "scalar_tensor_tensor", c.sm[smb][:],
                                    c.pb[sb_][:], INV_SQRT_T,
                                    c.mask[:, u * 512:(u + 1) * 512],
                                    ALU.mult, ALU.mult,
                                    waits=[ev_s, ev_mask] + W(ev_exp_par[smb]),
                                    inc=True)
                        ev_stt[sb_] = ev_sm
                        ev_e = sop(c, "activation", c.pt[:, u * 512:(u + 1) * 512],
                                   c.sm[smb][:], AF.Exp,
                                   waits=[ev_sm] + W(ev_pt_u.get(u)), inc=True)
                        ev_exp_par[smb] = ev_e
                    ev_exp_u[u] = ev_e
                if uu >= 4:
                    ui = uu - 4
                    u = comp[ui]
                    w = [ev_exp_u[u], ev_vh]
                    if ui == 0:
                        w += W(ev_pv_evac[h % 2])
                    last_pv = (ui == ncmp - 1) and not skip
                    ev_pv_stop = mm(c, c.pb[pvb][:],
                                    c.vh[bh][:, u * 128:(u + 1) * 128],
                                    c.pt[:, u * 512:(u + 1) * 512],
                                    start=(ui == 0), stop=last_pv, waits=w,
                                    inc=True)
                    ev_pt_u[u] = ev_pv_stop
                    wz = []
                    if ui == 0:
                        wz += W(ev_recip)
                    ev_z_stop = mm(c, c.pb[5][0:1, :], c.ones_bf[:],
                                   c.pt[:, u * 512:(u + 1) * 512],
                                   start=(ui == 0), stop=(ui == ncmp - 1), waits=wz,
                                   inc=(ui == ncmp - 1))
            if skip:
                ev_pv_stop = mm(c, c.pb[pvb][:], c.r_R[:], c.onesrow_bf[:],
                                start=False, stop=True,
                                waits=[ev_revac], inc=True)
                ev_rR_user = ev_pv_stop
            ev_z = ev_z_stop
            ev_kh_user[bh] = ev_z
            ev_vh_user[bh] = ("pe", ev_pv_stop[1])
            ev_qh_user[bh] = ev_z
            if skip:
                ev_zc = vop(c, "tensor_scalar_add", c.r_z, c.pb[5][0:1, :],
                            float(len(skip) * 128),
                            waits=[ev_z] + W(ev_bc_prev), inc=True)
            if skip:
                ev_rc = vop(c, "reciprocal", c.r_z, c.r_z,
                            waits=[ev_zc], inc=True)
            else:
                ev_rc = vop(c, "reciprocal", c.r_z, c.pb[5][0:1, :],
                            waits=[ev_z] + W(ev_bc_prev), inc=True)
            ev_recip = ev_rc
            wbc = [ev_rc] + W(ev_zbcopy)
            if skip:
                wbc.append(ev_revac)
            ev_bc = mm(c, c.pb[6][:], c.onesr_f[:], c.r_z,
                       start=True, stop=True, waits=wbc, inc=True)
            ev_bc_prev = ev_bc
            ev_zbc = sop(c, "copy", c.zb[:], c.pb[6][:],
                         waits=[ev_bc] + W(ev_zb_user), inc=True)
            ev_zbcopy = ev_zbc
            ev_ae = vop(c, "tensor_mul", c.a_all[:, h * 512:(h + 1) * 512],
                        c.pb[pvb][:], c.zb[:],
                        waits=[ev_pv_stop, ev_zbc] + W(ev_aall_war), inc=True)
            ev_pv_evac[h % 2] = ev_ae
            ev_zb_user = ev_ae
            ev_ae_all.append(ev_ae)
        if pending_tail is not None:
            pending_tail()
            pending_tail = None
        ev_b2_pe = r.mark("pe")
        ev_b2_dve = r.mark("dve")
        if DEBUG_DUMPS and tb == 0:
            dbg_dump(c, "daall", c.a_all[:], list(ev_ae_all))
            dbg_dump(c, "dpt", c.pt[:], [ev_b2_pe, r.mark("act")])

        bank_war0 = [ev_stt.get(0), ev_stt.get(1), ev_stt.get(2), ev_r3_war,
                     ev_pv_evac[0], ev_recip, ev_zbcopy, ev_pv_evac[1]]
        ev_wo_evac = [None] * ND
        for oh in range(2):
            for i in range(ND):
                bwo = i % 2
                wl = W(ev_wob_user[bwo])
                if tb == 0 and oh == 0 and i < 2:
                    wl.append(ev_phaseA_pe)
                if tb == 0 and oh == 0 and i < 2:
                    wl += ev_cast_wo
                ev_wol = dma(c, "scalar", f"qo{bwo}", c.wob[bwo][:],
                             d["woc"].ap()[i * 128:(i + 1) * 128,
                                           oh * 1024:(oh + 1) * 1024], waits=wl)
                for j in range(8):
                    w = [ev_wol, ev_ae_all[i]]
                    if i == 0:
                        w += W(bank_war0[j] if oh == 0 else ev_wo_evac[j])
                    e_ = mm(c, c.pb[j][:], c.wob[bwo][:, j * 128:(j + 1) * 128],
                            c.a_all[:, i * 512:(i + 1) * 512],
                            start=(i == 0), stop=(i == ND - 1), waits=w,
                            inc=(j == 7))
                ev_wob_user[bwo] = e_
            ev_oh = r.mark("pe")
            for j in range(8):
                ot = oh * 8 + j
                e = vop(c, "tensor_copy",
                        c.ao_all[:, ot * 512:(ot + 1) * 512],
                        c.pb[j][:],
                        waits=[ev_oh] + (W(*ev_ao_war) if isinstance(ev_ao_war, list)
                                         else W(ev_ao_war)), inc=True)
                ev_wo_evac[ot if oh == 0 else j] = e
                if oh == 1:
                    ev_wo_evac[ot] = e
        ev_aall_war = r.mark("pe")
        if DEBUG_DUMPS and tb == 0:
            dbg_dump(c, "dao", c.ao_all[:], [e for e in ev_wo_evac if e])

        xw = [ev_kh_user[0], ev_kh_user[1]]
        xw += W(*ev_xcown_war) if isinstance(ev_xcown_war, list) else W(ev_xcown_war)
        ev_xc = dma(c, "sync", "qxo", c.xc_own[:].rearrange("p (j t) -> p j t", j=ND),
                    xT3[:, :, tc0:tc0 + 512], waits=W(*xw))

        ev_mbc2, ev_rbc2, ev_rows2, ln_sq_user = ln_stats(
            c, [c.ao_all[:, j * 512:(j + 1) * 512] for j in range(ND)], 512,
            sum_bank=4, sq_bank=5, mbc_bank=6, rbc_bank=7,
            tile_waits=[[ev_wo_evac[j]] for j in range(ND)],
            war_banks=W(ev_wo_evac[4], ev_wo_evac[5], ev_wo_evac[12], ev_wo_evac[13]),
            war_bc=W(ev_wo_evac[6], ev_wo_evac[7], ev_wo_evac[14], ev_wo_evac[15],
                     ev_prev_ln4), sq_user_prev=ln_sq_user)
        ev_me = sop(c, "copy", c.mbs[:], c.pb[6][:],
                    waits=[ev_mbc2] + W(ev_exp_par[2]), inc=True)
        ev_re = sop(c, "copy", c.rbs[:], c.pb[7][:],
                    waits=[ev_rbc2] + W(ev_exp_par[3]), inc=True)
        ev_t1b = [None, None]
        ev_yw = []
        for j in range(ND):
            p_ = j % 2
            op = pop if j % 4 == 3 else vop
            et = op(c, "tensor_sub", c.t1[p_][:],
                    c.ao_all[:, j * 512:(j + 1) * 512], c.mbs[:],
                    waits=[ev_me] + W(ev_t1b[p_], ev_exp_par[p_]), inc=True)
            et2 = op(c, "tensor_mul", c.t1[p_][:], c.t1[p_][:], c.rbs[:],
                     waits=[et, ev_re], inc=True)
            ey = op(c, "tensor_add", c.y_all[:, j * 512:(j + 1) * 512],
                    c.t1[p_][:], c.xc_own[:, j * 512:(j + 1) * 512],
                    waits=[et2, ev_xc] + (W(*ev_y_war) if j == 0 else []),
                    inc=True)
            ev_t1b[p_] = ey
            ev_yw.append(ey)
        ev_xcown_war = ev_yw[-1]
        ev_y_war = []
        if DEBUG_DUMPS and tb == 0:
            dbg_dump(c, "dy", c.y_all[:], ev_yw)

        ev_mbc3, ev_rbc3, ev_rows3, ln_sq_user = ln_stats(
            c, [c.y_all[:, j * 512:(j + 1) * 512] for j in range(ND)], 512,
            sum_bank=4, sq_bank=5, mbc_bank=6, rbc_bank=7,
            tile_waits=[[ev_yw[j]] for j in range(ND)],
            war_banks=[ev_rows2],
            war_bc=[ev_t1b[0], ev_t1b[1]], sq_user_prev=ln_sq_user)
        ev_ln3_pe = r.mark("pe")
        ev_me3 = sop(c, "copy", c.mbs[:], c.pb[6][:],
                     waits=[ev_mbc3, ev_yw[2]], inc=True)
        ev_re3 = sop(c, "copy", c.rbs[:], c.pb[7][:],
                     waits=[ev_rbc3, ev_yw[3]], inc=True)
        ev_zw = []
        for j in range(ND):
            p_ = j % 2
            op = pop if j % 4 == 3 else vop
            et = op(c, "tensor_sub", c.t1[p_][:],
                    c.y_all[:, j * 512:(j + 1) * 512], c.mbs[:],
                    waits=[ev_me3] + W(ev_t1b[p_]), inc=True)
            ez = op(c, "tensor_mul", c.z_all[:, j * 512:(j + 1) * 512],
                    c.t1[p_][:], c.rbs[:],
                    waits=[et, ev_re3] + (W(ev_z_war) if j == 0 else []),
                    inc=True)
            ev_t1b[p_] = ez
            ev_zw.append(ez)
            heartbeat(c, ez)
        ev_z_done = ev_zw[-1]
        if DEBUG_DUMPS and tb == 0:
            dbg_dump(c, "dz", c.z_all[:], ev_zw)

        ev_silu = [None] * 4
        ev_hstt = [None] * 4
        ev_bank03_war = [ev_wo_evac[j] for j in range(4)]
        for ig in range(NIG):
            for ii in range(4):
                bg = ii % 2
                gl = W(ev_gbig_user[bg])
                if ig == 0 and ii < 2:
                    gl.append(ev_aall_war)
                if tb == 0 and ig == 0 and ii < 2:
                    gl += ev_cast_g
                ev_gl = dma(c, "sync", f"qg{bg}",
                            c.gbig[bg][:].rearrange("p (jj o) -> p jj o", jj=4),
                            d["gwc"].ap().rearrange("(j p) o -> p j o", p=128)[
                                :, ii * 4:(ii + 1) * 4, ig * 512:(ig + 1) * 512],
                            waits=gl)
                for jd in range(4):
                    i = ii * 4 + jd
                    for j in range(4):
                        w = [ev_gl, ev_zw[i]]
                        if i == 0:
                            w += W(ev_silu[j] if ig > 0 else ev_bank03_war[j])
                        e_ = mm(c, c.pb[j][:],
                                c.gbig[bg][:, jd * 512 + j * 128:jd * 512 + j * 128 + 128],
                                c.z_all[:, i * 512:(i + 1) * 512],
                                start=(i == 0), stop=(i == ND - 1), waits=w,
                                inc=(j == 3))
                ev_gbig_user[bg] = e_
            ev_g = r.mark("pe")
            for j in range(4):
                ev_silu[j] = sop(c, "activation", c.sg[j][:], c.pb[j][:],
                                 AF.Silu,
                                 scale=c.gs2_sb[:, ig * 4 + j: ig * 4 + j + 1],
                                 waits=[ev_g, ev_gs] + W(ev_hstt[j], ev_exp_par[j]),
                                 inc=True)
            for ii in range(4):
                bu = ii % 2
                ul = W(ev_ubig_user[bu])
                if ig == 0 and ii < 2:
                    ul.append(ev_aall_war)
                    ul.append(ev_rows3)
                if tb == 0 and ig == 0 and ii < 2:
                    ul += ev_cast_u
                ev_ul = dma(c, "sync", f"qu{bu}",
                            c.ubig[bu][:].rearrange("p (jj o) -> p jj o", jj=4),
                            d["uwc"].ap().rearrange("(j p) o -> p j o", p=128)[
                                :, ii * 4:(ii + 1) * 4, ig * 512:(ig + 1) * 512],
                            waits=ul)
                for jd in range(4):
                    i = ii * 4 + jd
                    for j in range(4):
                        w = [ev_ul, ev_zw[i]]
                        if i == 0:
                            if ig > 0:
                                w += W(ev_hstt[j])
                            else:
                                w += [ev_rows3] if j < 2 else [ev_zw[-1]]
                        e_ = mm(c, c.pb[4 + j][:],
                                c.ubig[bu][:, jd * 512 + j * 128:jd * 512 + j * 128 + 128],
                                c.z_all[:, i * 512:(i + 1) * 512],
                                start=(i == 0), stop=(i == ND - 1), waits=w,
                                inc=(j == 3))
                ev_ubig_user[bu] = e_
            ev_u = r.mark("pe")
            for j in range(4):
                it = ig * 4 + j
                w = [ev_u, ev_silu[j], ev_us]
                if ig == 0 and j == 0:
                    w += W(ev_hall_war, ev_b2_dve)
                ev_hstt[j] = vop(c, "scalar_tensor_tensor",
                                 c.h_all[:, it * 512:(it + 1) * 512],
                                 c.pb[4 + j][:], c.us2_sb[:, it:it + 1],
                                 c.sg[j][:], ALU.mult, ALU.mult,
                                 waits=w, inc=True)
        ev_z_war = r.mark("pe")
        ev_h_done = ev_hstt[3]
        if DEBUG_DUMPS and tb == 0:
            dbg_dump(c, "dh", c.h_all[:], [ev_hstt[j] for j in range(4)])

        ev_dbank_war = list(ev_hstt)
        ev_f = []
        for og in range(4):
            for ii in range(22):
                bd = ii % 2
                dl = W(ev_dbig_user[bd])
                if og == 0 and ii < 2:
                    dl.append(ev_ln3_pe)
                if tb == 0 and og == 0 and ii < 2:
                    dl += ev_cast_d
                ev_dl = dma(c, "scalar", f"qd{bd}",
                            c.dbig[bd][:].rearrange("p (jj o) -> p jj o", jj=2),
                            d["dwc"].ap().rearrange("(j p) o -> p j o", p=128)[
                                :, ii * 2:(ii + 1) * 2, og * 512:(og + 1) * 512],
                            waits=dl)
                for jd in range(2):
                    i = ii * 2 + jd
                    for j in range(4):
                        w = [ev_dl, ev_h_done]
                        if i == 0:
                            w += W(ev_dbank_war[j])
                        e_ = mm(c, c.pb[j][:],
                                c.dbig[bd][:, jd * 512 + j * 128:jd * 512 + j * 128 + 128],
                                c.h_all[:, i * 512:(i + 1) * 512],
                                start=(i == 0), stop=(i == NI - 1), waits=w,
                                inc=(j == 3))
                ev_dbig_user[bd] = e_
            ev_dg = r.mark("pe")
            for j in range(4):
                ot = og * 4 + j
                w = [ev_dg, ev_ds]
                if og == 0 and j == 0:
                    w += (W(*ev_ao_war) if isinstance(ev_ao_war, list)
                          else W(ev_ao_war))
                e = vop(c, "tensor_scalar_mul",
                        c.ao_all[:, ot * 512:(ot + 1) * 512], c.pb[j][:],
                        c.ds2_sb[:, ot:ot + 1], waits=w, inc=True)
                ev_dbank_war[j] = e
                ev_f.append(e)
        ev_hall_war = r.mark("pe")
        if DEBUG_DUMPS and tb == 0:
            dbg_dump(c, "df", c.ao_all[:], ev_f)

        ev_mbc4, ev_rbc4, ev_rows4, ln_sq_user = ln_stats(
            c, [c.ao_all[:, j * 512:(j + 1) * 512] for j in range(ND)], 512,
            sum_bank=4, sq_bank=5, mbc_bank=6, rbc_bank=7,
            tile_waits=[[ev_f[j]] for j in range(ND)],
            war_banks=[ev_hstt[0], ev_hstt[1]],
            war_bc=W(ev_hstt[2], ev_hstt[3], ev_zw[-1]),
            sq_user_prev=ln_sq_user)
        ev_me4 = sop(c, "copy", c.mbs[:], c.pb[6][:],
                     waits=[ev_mbc4] + W(ev_hstt[2]), inc=True)
        ev_re4 = sop(c, "copy", c.rbs[:], c.pb[7][:],
                     waits=[ev_rbc4] + W(ev_hstt[3]), inc=True)
        ev_t1b = [None, None]
        ev_outw = []
        for j in range(ND):
            p_ = j % 2
            op = pop if j % 4 == 3 else vop
            et = op(c, "tensor_sub", c.t1[p_][:],
                    c.ao_all[:, j * 512:(j + 1) * 512], c.mbs[:],
                    waits=[ev_me4] + W(ev_t1b[p_]), inc=True)
            et2 = op(c, "tensor_mul", c.t1[p_][:], c.t1[p_][:], c.rbs[:],
                     waits=[et] + W(ev_re4), inc=True)
            eo = op(c, "tensor_add", c.y_all[:, j * 512:(j + 1) * 512],
                    c.t1[p_][:], c.y_all[:, j * 512:(j + 1) * 512],
                    waits=[et2], inc=True)
            ev_t1b[p_] = eo
            ev_outw.append(eo)
            heartbeat(c, eo)
        ev_xcown_war = ev_outw[-1]
        ev_y_war = [ev_out_store]
        ev_ao_war = ev_outw[-1]
        for u_ in list(ev_pt_u):
            ev_pt_u[u_] = None
        ev_pt_u[0] = ev_outw[-1]
        ev_pt_u[1] = ev_outw[-1]
        for k_ in range(NS):
            ev_pt_u[k_] = ev_outw[-1]

        ev_out_store = dma(
            c, "sync", "qout",
            d["outT"].ap().rearrange("(j p) t -> p j t", p=128)[:, :, tc0:tc0 + 512],
            c.y_all[:].rearrange("p (j t) -> p j t", j=ND), waits=ev_outw)
        ev_mask_war = [ev_hall_war, r.mark("dve"), r.mark("pool")]
        ev_prev_ln4 = ev_outw[-1]
        ev_exp_par = [ev_outw[ND - 2], ev_outw[ND - 1],
                      ev_outw[ND - 2], ev_outw[ND - 1]]
        c.r.wait("vector", ev_outw[ND - 1])
        c.r.wait("scalar", ev_outw[ND - 1])
        for j_ in range(4):
            ev_stt[j_] = ev_dbank_war[j_]
        ev_r3_war = ev_dbank_war[3]

    r.wait("sync", r.mark("qout"))
    if DEBUG_DUMPS:
        r.wait("sync", r.mark("qdbg"))



_NC_CACHE = {}


def kernel(L0_attn_x, L0_attn_Wq, L0_attn_Wk, L0_attn_Wv, L0_attn_Wo,
           L0_ffn_gate_w, L0_ffn_up_w, L0_ffn_down_w,
           L0_ffn_gate_s, L0_ffn_up_s, L0_ffn_down_s):
    x = np.asarray(L0_attn_x, dtype=np.float32)
    wq = np.ascontiguousarray(np.asarray(L0_attn_Wq, dtype=np.float32))
    wk = np.ascontiguousarray(np.asarray(L0_attn_Wk, dtype=np.float32))
    wv = np.ascontiguousarray(np.asarray(L0_attn_Wv, dtype=np.float32))
    wo = np.ascontiguousarray(np.asarray(L0_attn_Wo, dtype=np.float32))
    gwT = np.ascontiguousarray(np.asarray(L0_ffn_gate_w, dtype=np.float32).T)
    uwT = np.ascontiguousarray(np.asarray(L0_ffn_up_w, dtype=np.float32).T)
    dwT = np.ascontiguousarray(np.asarray(L0_ffn_down_w, dtype=np.float32).T)
    gs2 = np.ascontiguousarray(np.asarray(L0_ffn_gate_s, dtype=np.float32).reshape(NI, 128).T)
    us2 = np.ascontiguousarray(np.asarray(L0_ffn_up_s, dtype=np.float32).reshape(NI, 128).T)
    ds2 = np.ascontiguousarray(np.asarray(L0_ffn_down_s, dtype=np.float32).reshape(ND, 128).T)

    if "nc" not in _NC_CACHE:
        _NC_CACHE["nc"] = build_nc()
    nc = _NC_CACHE["nc"]

    in_maps = []
    for core in range(8):
        b, th = core // 2, core % 2
        blocks = np.arange(4096).reshape(16, 256)
        perm = np.concatenate([blocks[th::2].ravel(), blocks[1 - th::2].ravel()])
        xTc = np.ascontiguousarray(x[b][perm, :].T)
        maskc = (perm[:, None] <= perm[None, :2048]).astype(np.float32)
        in_maps.append(dict(xT=xTc, maskT=maskc, wq=wq, wk=wk, wv=wv, wo=wo,
                            gwT=gwT, uwT=uwT, dwT=dwT, gs2=gs2, us2=us2,
                            ds2=ds2))

    res = run_bass_kernel_spmd(nc, in_maps, core_ids=list(range(8)))

    out = np.empty((4, 4096, 2048), dtype=np.float32)
    for core in range(8):
        b, th = core // 2, core % 2
        blocks = np.arange(4096).reshape(16, 256)
        own = blocks[th::2].ravel()
        out[b, own, :] = res.results[core]["outT"].T
    return out


if __name__ == "__main__":
    nc = build_nc()
    print("built ok; instructions:", nc.n_instructions())
